# version 26
# baseline (speedup 1.0000x reference)
import contextlib
import numpy as np

import concourse.bass as bass
import concourse.mybir as mybir
from concourse.bass_utils import run_bass_kernel_spmd

F32 = mybir.dt.float32
BF16 = mybir.dt.bfloat16
I32 = mybir.dt.int32
AF = mybir.ActivationFunctionType
OP = mybir.AluOpType
AX = mybir.AxisListType

N = 4096
C = 1024
M3 = 3 * C
H = 4096
NT = N // 128
T3 = 256
NT3 = N // T3
EPS_SUM = 1e-7
EPS_NORM = 1e-12
LN_EPS = 1e-5
QCLIP = 1e-5
INV127 = 1.0 / 127.0
INV64 = 1.0 / 64.0
INV1024 = 1.0 / 1024.0


_DEBUG_SCRATCH = False


def sem_scope(nc, es):
    handles = []

    def closer():
        nc.clear_and_free_semaphores(handles)
        nc.all_engine_barrier()

    es.callback(closer)

    def alloc(name):
        h = nc.alloc_semaphore(name=name)
        handles.append(h)
        return h

    return alloc


class SelfSync:

    def __init__(self, eng, sem):
        self.eng, self.sem, self.n = eng, sem, 0

    def sync(self, ins):
        self.n += 1
        ins.then_inc(self.sem, 1)
        self.eng.wait_ge(self.sem, self.n)


def build_nc():
    nc = bass.Bass("TRN2", target_bir_lowering=False,
                   detect_race_conditions=False)

    x_h = nc.dram_tensor("x", [N, C], F32, kind="ExternalInput")
    wqkv_h = nc.dram_tensor("wqkv", [C, M3], F32, kind="ExternalInput")
    bqkv_h = nc.dram_tensor("bqkv", [1, M3], F32, kind="ExternalInput")
    wln_h = nc.dram_tensor("wln", [C, C], F32, kind="ExternalInput")
    bln_h = nc.dram_tensor("bln", [1, C], F32, kind="ExternalInput")
    w1_h = nc.dram_tensor("w1", [H, C], F32, kind="ExternalInput")
    cw_h = nc.dram_tensor("cw", [128, 96], F32, kind="ExternalInput")
    cb_h = nc.dram_tensor("cb", [128, 32], F32, kind="ExternalInput")
    w2_h = nc.dram_tensor("w2", [C, H], F32, kind="ExternalInput")
    out_h = nc.dram_tensor("out", [N, C], F32, kind="ExternalOutput")

    _skind = "Internal" if not _DEBUG_SCRATCH else "ExternalOutput"
    q2f_s = nc.dram_tensor("q2f_s", [N, C], BF16, kind=_skind)
    k2n_s = nc.dram_tensor("k2n_s", [N, C], BF16, kind=_skind)
    v_s = nc.dram_tensor("v_s", [N, C], BF16, kind=_skind)
    x2_s = nc.dram_tensor("x2_s", [N, C], F32, kind=_skind)
    xqT_s = nc.dram_tensor("xqT_s", [C, N + 2], BF16, kind=_skind)
    t1T_s = nc.dram_tensor("t1T_s", [C, H], BF16, kind=_skind)

    with contextlib.ExitStack() as es1:
        sem1 = sem_scope(nc, es1)
        ident_b = es1.enter_context(nc.sbuf_tensor("ident_b", [128, 128], BF16))
        ident_f = es1.enter_context(nc.sbuf_tensor("ident_f", [128, 128], F32))
        ones_col_b = es1.enter_context(nc.sbuf_tensor("ones_col_b", [128, 1], BF16))
        ones_col_f = es1.enter_context(nc.sbuf_tensor("ones_col_f", [128, 1], F32))
        ones_row_b = es1.enter_context(nc.sbuf_tensor("ones_row_b", [1, 128], BF16))
        ones_row_f = es1.enter_context(nc.sbuf_tensor("ones_row_f", [1, 128], F32))
        m1_col = es1.enter_context(nc.sbuf_tensor("m1_col", [128, 1], F32))
        eps_col = es1.enter_context(nc.sbuf_tensor("eps_col", [128, 1], F32))
        m2_col = es1.enter_context(nc.sbuf_tensor("m2_col", [128, 1], F32))
        dq_row = es1.enter_context(nc.sbuf_tensor("dq_row", [1, N + 2], F32))
        rcol = es1.enter_context(nc.sbuf_tensor("rcol", [128, 8], F32))
        t2w_sb = es1.enter_context(nc.sbuf_tensor("t2w_sb", [128, 32 * 1024], BF16))
        cw_sb = es1.enter_context(nc.sbuf_tensor("cw_sb", [128, 96], F32))
        cb_sb = es1.enter_context(nc.sbuf_tensor("cb_sb", [128, 32], F32))
        zero8_b = es1.enter_context(nc.sbuf_tensor("zero8_b", [128, 8], BF16))
        s_p0 = sem1("s_p0")
        s_p0g = sem1("s_p0g")
        s_p0s = sem1("s_p0s")
        with nc.Block() as blk:
            @blk.gpsimd
            def _(g):
                g.memset(ident_b[:, :], 0.0).then_inc(s_p0, 1)
                g.wait_ge(s_p0, 1)
                g.affine_select(out=ident_b[:, :], in_=ident_b[:, :],
                                compare_op=OP.not_equal, fill=1.0,
                                base=0, pattern=[[-1, 128]], channel_multiplier=1)
                g.memset(ident_f[:, :], 0.0).then_inc(s_p0, 1)
                g.wait_ge(s_p0, 2)
                g.affine_select(out=ident_f[:, :], in_=ident_f[:, :],
                                compare_op=OP.not_equal, fill=1.0,
                                base=0, pattern=[[-1, 128]], channel_multiplier=1)
                g.memset(ones_col_b[:, :], 1.0)
                g.memset(ones_col_f[:, :], 1.0)
                g.memset(ones_row_b[:, :], 1.0)
                g.memset(ones_row_f[:, :], 1.0)
                g.memset(zero8_b[:, :], 0.0).then_inc(s_p0, 1)
                g.memset(dq_row[:, :], 0.0)
                g.memset(eps_col[:, :], LN_EPS)
                g.wait_ge(s_p0, 3)
                g.dma_start(
                    bass.AP(xqT_s, 0, [[N + 2, 128], [128 * (N + 2), 8], [1, 1]]),
                    bass.AP(zero8_b, 0, [[8, 128], [1, 8], [1, 1]]),
                    allow_slow_non_contiguous=True,
                ).then_inc(s_p0g, 16)
                g.dma_start(
                    bass.AP(xqT_s, N + 1, [[N + 2, 128], [128 * (N + 2), 8], [1, 1]]),
                    bass.AP(zero8_b, 0, [[8, 128], [1, 8], [1, 1]]),
                    allow_slow_non_contiguous=True,
                ).then_inc(s_p0g, 16)
                g.wait_ge(s_p0g, 32)

            @blk.sync
            def _(sy):
                sy.dma_start(cw_sb[:, :], cw_h[:, :]).then_inc(s_p0s, 16)
                sy.dma_start(cb_sb[:, :], cb_h[:, :]).then_inc(s_p0s, 16)
                sy.wait_ge(s_p0s, 32)

        def emit_wquant_jobs(jobs):
            stack = []
            semh = []
            def mk(f, *a):
                h = f(*a); r = h.__enter__(); stack.append(h); return r
            def mks(name):
                h = nc.alloc_semaphore(name=name)
                semh.append(h)
                return h
            for jb in jobs:
                tag, kd = jb["tag"], jb["kd"]
                jb["nb"] = kd // 128
                jb["wf"] = mk(nc.sbuf_tensor, f"{tag}_wf", [128, 3 * kd], F32)
                jb["a1m"] = mk(nc.sbuf_tensor, f"{tag}_a1m", [128, kd], F32)
                jb["tern"] = mk(nc.sbuf_tensor, f"{tag}_tern",
                                [128, 2 * kd], BF16)
                jb["csum"] = mk(nc.sbuf_tensor, f"{tag}_csum", [128, 4], F32)
                jb["acc"] = mk(nc.sbuf_tensor, f"{tag}_acc", [128, 1], F32)
                jb["msb"] = mk(nc.sbuf_tensor, f"{tag}_msb", [1, 1], F32)
                jb["thr"] = mk(nc.sbuf_tensor, f"{tag}_thr", [128, 2], F32)
                jb["stage"] = (mk(nc.sbuf_tensor, f"{tag}_stage",
                                  [128, 2 * kd], BF16)
                               if jb["sink"] == "dma" else None)
                jb["ps_tr"] = mk(nc.psum_tensor, f"{tag}_ptr", [128, kd], BF16)
                jb["s_ld"] = [mks(f"{tag}_ld0"), mks(f"{tag}_ld1"),
                              mks(f"{tag}_ld2")]
                jb["s_ab"] = mks(f"{tag}_ab")
                jb["s_ac"] = mks(f"{tag}_ac")
                jb["s_qd"] = mks(f"{tag}_qd")
                jb["s_tr"] = mks(f"{tag}_tr")
                jb["s_ev"] = mks(f"{tag}_ev")
                jb["s_dm"] = [mks(f"{tag}_dm0"), mks(f"{tag}_dm1")]
            ps_m = mk(nc.psum_tensor, "wq_pm", [1, 1], F32)
            ps_b = mk(nc.psum_tensor, "wq_pb", [128, 1], F32)
            s_mn = mks("wq_mn")
            s_tm = mks("wq_tm")

            sched = []
            for k in range(8):
                sched.append((jobs[0], k))
                for q in range(4):
                    sched.append((jobs[1], 4 * k + q))

            def all_loads(eng, jb):
                nch, kd = jb["nch"], jb["kd"]
                for i in range(nch):
                    if i >= 3:
                        eng.wait_ge(jb["s_ab"], i - 2)
                    eng.dma_start(
                        jb["wf"][:, (i % 3) * kd:(i % 3 + 1) * kd],
                        jb["src"][i * 128:(i + 1) * 128, :]
                    ).then_inc(jb["s_ld"][i % 3], 16)
                for j in range(nch):
                    k = nch + j
                    if k - 3 < nch:
                        eng.wait_ge(jb["s_ab"], k - 2)
                    if j >= 3:
                        eng.wait_ge(jb["s_qd"], j - 2)
                    eng.dma_start(
                        jb["wf"][:, (k % 3) * kd:(k % 3 + 1) * kd],
                        jb["src"][j * 128:(j + 1) * 128, :]
                    ).then_inc(jb["s_ld"][k % 3], 16)

            with nc.Block() as blk:
                @blk.sync
                def _(sy):
                    all_loads(sy, jobs[0])

                @blk.gpsimd
                def _(g):
                    all_loads(g, jobs[1])

                @blk.scalar
                def _(sc):
                    for jb, i in sched:
                        kd = jb["kd"]
                        sc.wait_ge(jb["s_ld"][i % 3], 16 * (i // 3 + 1))
                        if i >= 4:
                            sc.wait_ge(jb["s_ac"], i - 3)
                        sc.activation(jb["a1m"][:, :],
                                      jb["wf"][:, (i % 3) * kd:(i % 3 + 1) * kd],
                                      AF.Abs,
                                      accum_out=jb["csum"][:, i % 4:i % 4 + 1]
                                      ).then_inc(jb["s_ab"], 1)
                    jb = jobs[1]
                    kd, nb, nch = jb["kd"], jb["nb"], jb["nch"]
                    for i in range(nch):
                        sc.wait_ge(jb["s_ev"], i + 1)
                        sc.dma_start(
                            bass.AP(t1T_s, i * 128,
                                    [[H, 128], [128 * H, nb], [1, 128]]),
                            bass.AP(jb["stage"], (i % 2) * kd,
                                    [[2 * kd, 128], [128, nb], [1, 128]]),
                        ).then_inc(jb["s_dm"][i % 2], 16)
                    sc.wait_ge(jb["s_dm"][0], 16 * ((nch + 1) // 2))
                    sc.wait_ge(jb["s_dm"][1], 16 * (nch // 2))

                @blk.tensor
                def _(t):
                    t.wait_ge(jobs[0]["s_ac"], jobs[0]["nch"])
                    t.matmul(ps_m[0:1, 0:1], jobs[0]["acc"][:, :],
                             ones_col_f[:, :], start=True,
                             stop=True).then_inc(s_tm, 1)
                    t.wait_ge(jobs[1]["s_ac"], jobs[1]["nch"])
                    t.wait_ge(s_mn, 1)
                    t.matmul(ps_m[0:1, 0:1], jobs[1]["acc"][:, :],
                             ones_col_f[:, :], start=True,
                             stop=True).then_inc(s_tm, 1)
                    t.matmul(ps_b[:, 0:1], ones_row_f[0:1, :],
                             jobs[0]["msb"][0:1, 0:1], start=True,
                             stop=True).then_inc(s_tm, 1)
                    t.wait_ge(s_mn, 3)
                    t.matmul(ps_b[:, 0:1], ones_row_f[0:1, :],
                             jobs[1]["msb"][0:1, 0:1], start=True,
                             stop=True).then_inc(s_tm, 1)
                    for jb, i in sched:
                        kd, nb = jb["kd"], jb["nb"]
                        t.wait_ge(jb["s_qd"], i + 1)
                        if i >= 1:
                            t.wait_ge(jb["s_ev"], i)
                        ins = None
                        for b in range(nb):
                            ins = t.transpose(
                                jb["ps_tr"][:, b * 128:(b + 1) * 128],
                                jb["tern"][:, (i % 2) * kd + b * 128:
                                           (i % 2) * kd + (b + 1) * 128],
                                ident_b[:, :])
                        ins.then_inc(jb["s_tr"], 1)

                @blk.vector
                def _(v):
                    for jb, i in sched:
                        v.wait_ge(jb["s_ab"], i + 1)
                        if i == 0:
                            ins = v.tensor_copy(jb["acc"][:, :],
                                                jb["csum"][:, 0:1])
                        else:
                            ins = v.tensor_tensor(jb["acc"][:, :],
                                                  jb["acc"][:, :],
                                                  jb["csum"][:, i % 4:i % 4 + 1],
                                                  OP.add)
                        ins.then_inc(jb["s_ac"], 1)
                        v.wait_ge(jb["s_ac"], i + 1)
                    for jx, jb in enumerate(jobs):
                        v.wait_ge(s_tm, jx + 1)
                        v.tensor_copy(jb["msb"][0:1, 0:1], ps_m[0:1, 0:1]
                                      ).then_inc(s_mn, 1)
                    for jx, jb in enumerate(jobs):
                        v.wait_ge(s_tm, jx + 3)
                        v.tensor_scalar(jb["mcol"][:, :], ps_b[:, 0:1],
                                        1.0 / (jb["nch"] * 128.0 * jb["kd"]),
                                        QCLIP, OP.mult, OP.max
                                        ).then_inc(s_mn, 1)
                        v.wait_ge(s_mn, 2 * jx + 3)
                        v.tensor_scalar(jb["thr"][:, 0:1], jb["mcol"][:, :],
                                        0.5, None, OP.mult, OP.bypass)
                        v.tensor_scalar(jb["thr"][:, 1:2], jb["mcol"][:, :],
                                        -0.5, None, OP.mult, OP.bypass
                                        ).then_inc(s_mn, 1)
                        v.wait_ge(s_mn, 2 * jx + 4)
                    def evac(jb, k):
                        kd, nb = jb["kd"], jb["nb"]
                        v.wait_ge(jb["s_tr"], k + 1)
                        if jb["sink"] == "resident":
                            v.tensor_copy(
                                bass.AP(t2w_sb, k * 128,
                                        [[32 * 1024, 128], [1024, 32],
                                         [1, 128]]),
                                bass.AP(jb["ps_tr"], 0,
                                        [[kd, 128], [128, nb], [1, 128]]),
                            ).then_inc(jb["s_ev"], 1)
                        else:
                            if k >= 2:
                                v.wait_ge(jb["s_dm"][k % 2], 16 * (k // 2))
                            v.tensor_copy(
                                jb["stage"][:, (k % 2) * kd:(k % 2 + 1) * kd],
                                jb["ps_tr"][:, :]).then_inc(jb["s_ev"], 1)

                    for jb, j in sched:
                        kd, nch = jb["kd"], jb["nch"]
                        k = nch + j
                        ca = len([x for x in range(nch) if x % 3 == k % 3])
                        v.wait_ge(jb["s_ld"][k % 3], 16 * (ca + j // 3 + 1))
                        if j >= 2:
                            v.wait_ge(jb["s_tr"], j - 1)
                        src = jb["wf"][:, (k % 3) * kd:(k % 3 + 1) * kd]
                        v.tensor_scalar(jb["a1m"][:, :], src,
                                        jb["thr"][:, 0:1], -1.0,
                                        OP.is_gt, OP.add)
                        v.scalar_tensor_tensor(
                            jb["tern"][:, (j % 2) * kd:(j % 2 + 1) * kd], src,
                            jb["thr"][:, 1:2], jb["a1m"][:, :],
                            OP.is_ge, OP.add).then_inc(jb["s_qd"], 1)
                        if j >= 1:
                            evac(jb, j - 1)
                    for jb in jobs:
                        evac(jb, jb["nch"] - 1)

            nc.clear_and_free_semaphores(semh)
            nc.all_engine_barrier()
            for h in reversed(stack):
                h.__exit__(None, None, None)

        emit_wquant_jobs([
            {"tag": "w2q", "src": w2_h, "nch": 8, "kd": H,
             "mcol": m2_col, "sink": "resident"},
            {"tag": "w1q", "src": w1_h, "nch": 32, "kd": C,
             "mcol": m1_col, "sink": "dma"},
        ])

        with contextlib.ExitStack() as es2:
            sem2 = sem_scope(nc, es2)
            wq_sb = es2.enter_context(nc.sbuf_tensor("wq_sb", [128, 8 * M3], BF16))
            bqkv_r = es2.enter_context(nc.sbuf_tensor("bqkv_r", [1, M3], BF16))
            x_bf = es2.enter_context(nc.sbuf_tensor("x_bf", [128, 2 * C], BF16))
            xT_sb = es2.enter_context(nc.sbuf_tensor("xT_sb", [128, 2 * C], BF16))
            q_f = es2.enter_context(nc.sbuf_tensor("q_f", [128, 2 * C], F32))
            k_f = es2.enter_context(nc.sbuf_tensor("k_f", [128, 2 * C], F32))
            q2_f = es2.enter_context(nc.sbuf_tensor("q2_f", [128, 2 * C], F32))
            k2_f = es2.enter_context(nc.sbuf_tensor("k2_f", [128, 2 * C], F32))
            v_bf = es2.enter_context(nc.sbuf_tensor("v_bf", [128, 2 * C], BF16))
            q2f_bf = es2.enter_context(nc.sbuf_tensor("q2f_bf", [128, 2 * C], BF16))
            k2n_bf = es2.enter_context(nc.sbuf_tensor("k2n_bf", [128, 2 * C], BF16))
            k2sq_bf = es2.enter_context(nc.sbuf_tensor("k2sq_bf", [128, 2 * C], BF16))
            stat = es2.enter_context(nc.sbuf_tensor("stat", [128, 32], F32))
            ps_xT = es2.enter_context(nc.psum_tensor("ps_xT", [128, 2 * C], BF16))
            ps_q0 = es2.enter_context(nc.psum_tensor("ps_q0", [128, 512], F32))
            ps_q1 = es2.enter_context(nc.psum_tensor("ps_q1", [128, 512], F32))
            ps_q2 = es2.enter_context(nc.psum_tensor("ps_q2", [128, 512], F32))
            ps_rc0 = es2.enter_context(nc.psum_tensor("ps_rc0", [1, 512], F32))
            ps_rc1 = es2.enter_context(nc.psum_tensor("ps_rc1", [1, 512], F32))
            ps_rcT = es2.enter_context(nc.psum_tensor("ps_rcT", [128, 8], F32))
            rrow = es2.enter_context(nc.sbuf_tensor("rrow", [1, C], F32))
            s_ldw = sem2("a_ldw")
            s_ldx = [sem2("a_ldx0"),
                     sem2("a_ldx1")]
            s_xtr = sem2("a_xtr")
            s_xev = sem2("a_xev")
            s_mmq = sem2("a_mmq")
            s_qev = sem2("a_qev")
            s_qst = sem2("a_qst")
            s_sqs = sem2("a_sqs")
            s_fin = sem2("a_fin")
            s_spq = [sem2("a_spq0"),
                     sem2("a_spq1")]
            s_spk = [sem2("a_spk0"),
                     sem2("a_spk1")]
            s_spv = [sem2("a_spv0"),
                     sem2("a_spv1")]
            s_rcm = sem2("a_rcm")
            s_vss = sem2("a_vss")
            s_vs2 = sem2("a_vs2")
            ps_qkv = [ps_q0, ps_q1, ps_q2]
            with nc.Block() as blk:
                @blk.sync
                def _(sy):
                    for ip in range(NT):
                        pp = ip % 2
                        sy.wait_ge(s_fin, 3 * ip + 1)
                        sy.dma_start(q2f_s[ip * 128:(ip + 1) * 128, :],
                                     q2f_bf[:, pp * C:(pp + 1) * C]
                                     ).then_inc(s_spq[pp], 16)
                        sy.wait_ge(s_fin, 3 * ip + 3)
                        sy.dma_start(k2n_s[ip * 128:(ip + 1) * 128, :],
                                     k2n_bf[:, pp * C:(pp + 1) * C]
                                     ).then_inc(s_spk[pp], 16)
                        sy.wait_ge(s_qev, 6 * ip + 6)
                        sy.dma_start(v_s[ip * 128:(ip + 1) * 128, :],
                                     v_bf[:, pp * C:(pp + 1) * C]
                                     ).then_inc(s_spv[pp], 16)
                    for sm in (s_spq, s_spk, s_spv):
                        sy.wait_ge(sm[0], 16 * (NT // 2))
                        sy.wait_ge(sm[1], 16 * (NT // 2))

                @blk.gpsimd
                def _(g):
                    for j in range(8):
                        g.dma_start(wq_sb[:, j * M3:(j + 1) * M3],
                                    wqkv_h[j * 128:(j + 1) * 128, :]
                                    ).then_inc(s_ldw, 16)
                    g.dma_start(bqkv_r[:, :], bqkv_h[:, :]).then_inc(s_ldw, 16)
                    for i in range(NT):
                        if i >= 2:
                            g.wait_ge(s_xtr, i - 1)
                        g.dma_start(x_bf[:, (i % 2) * C:(i % 2 + 1) * C],
                                    x_h[i * 128:(i + 1) * 128, :]
                                    ).then_inc(s_ldx[i % 2], 16)

                @blk.tensor
                def _(t):
                    def rc_mm(k):
                        pk = k % 2
                        t.wait_ge(s_fin, 3 * k + 3)
                        t.matmul(ps_rc0[0:1, 0:512], ones_col_b[:, :],
                                 k2sq_bf[:, pk * C:pk * C + 512],
                                 start=(k == 0), stop=(k == NT - 1))
                        t.matmul(ps_rc1[0:1, 0:512], ones_col_b[:, :],
                                 k2sq_bf[:, pk * C + 512:pk * C + 1024],
                                 start=(k == 0), stop=(k == NT - 1)
                                 ).then_inc(s_rcm, 1)

                    t.wait_ge(s_ldw, 16 * 9)
                    for i in range(NT):
                        par = i % 2
                        t.wait_ge(s_ldx[par], 16 * (i // 2 + 1))
                        if i >= 2:
                            t.wait_ge(s_xev, i - 1)
                        ins = None
                        for b in range(8):
                            ins = t.transpose(ps_xT[:, par * C + b * 128:
                                                    par * C + (b + 1) * 128],
                                              x_bf[:, par * C + b * 128:
                                                   par * C + (b + 1) * 128],
                                              ident_b[:, :])
                        ins.then_inc(s_xtr, 1)
                        t.wait_ge(s_xev, i + 1)
                        for p in range(6):
                            gidx = 6 * i + p
                            if gidx >= 3:
                                t.wait_ge(s_qev, gidx - 2)
                            pb = ps_qkv[p % 3]
                            for cch in range(8):
                                t.matmul(pb[:, 0:512],
                                         xT_sb[:, par * C + cch * 128:
                                               par * C + (cch + 1) * 128],
                                         wq_sb[:, cch * M3 + p * 512:
                                               cch * M3 + (p + 1) * 512],
                                         start=(cch == 0), stop=False)
                            t.matmul(pb[:, 0:512], ones_row_b[0:1, :],
                                     bqkv_r[0:1, p * 512:(p + 1) * 512],
                                     start=False, stop=True).then_inc(s_mmq, 1)
                        if i >= 1:
                            rc_mm(i - 1)
                    rc_mm(NT - 1)

                @blk.vector
                def _(v):
                    vss = SelfSync(v, s_vss)
                    for i in range(NT):
                        par = i % 2
                        sb = 16 * par
                        for p in range(6):
                            v.wait_ge(s_mmq, 6 * i + p + 1)
                            pb = ps_qkv[p % 3]
                            if p < 2:
                                if i >= 2 and p == 0:
                                    v.wait_ge(s_sqs, 4 * (i - 2) + 2)
                                v.tensor_copy(q_f[:, par * C + p * 512:
                                                  par * C + (p + 1) * 512],
                                              pb[:, 0:512]).then_inc(s_qev, 1)
                            elif p < 4:
                                if i >= 2 and p == 2:
                                    v.wait_ge(s_sqs, 4 * (i - 2) + 4)
                                v.tensor_copy(k_f[:, par * C + (p - 2) * 512:
                                                  par * C + (p - 1) * 512],
                                              pb[:, 0:512]).then_inc(s_qev, 1)
                            else:
                                if i >= 2 and p == 4:
                                    v.wait_ge(s_spv[par], 16 * (i // 2))
                                v.tensor_copy(v_bf[:, par * C + (p - 4) * 512:
                                                   par * C + (p - 3) * 512],
                                              pb[:, 0:512]).then_inc(s_qev, 1)
                        vss.sync(v.reduce_sum(stat[:, sb + 0:sb + 1],
                                              q_f[:, par * C:(par + 1) * C],
                                              AX.X))
                        v.tensor_scalar(stat[:, sb + 1:sb + 2],
                                        stat[:, sb + 0:sb + 1], -INV1024, None,
                                        OP.mult, OP.bypass).then_inc(s_qst, 1)
                        vss.sync(v.reduce_sum(stat[:, sb + 11:sb + 12],
                                              k_f[:, par * C:(par + 1) * C],
                                              AX.X))
                        v.tensor_scalar(stat[:, sb + 12:sb + 13],
                                        stat[:, sb + 11:sb + 12], -INV1024, None,
                                        OP.mult, OP.bypass).then_inc(s_qst, 1)
                        v.wait_ge(s_sqs, 4 * i + 3)
                        vss.sync(v.tensor_scalar(stat[:, sb + 5:sb + 6],
                                                 stat[:, sb + 2:sb + 3],
                                                 EPS_SUM, None,
                                                 OP.add, OP.bypass))
                        vss.sync(v.reciprocal(stat[:, sb + 6:sb + 7],
                                              stat[:, sb + 5:sb + 6]))
                        vss.sync(v.tensor_tensor(stat[:, sb + 7:sb + 8],
                                                 stat[:, sb + 4:sb + 5],
                                                 stat[:, sb + 6:sb + 7],
                                                 OP.mult))
                        vss.sync(v.tensor_scalar(stat[:, sb + 8:sb + 9],
                                                 stat[:, sb + 7:sb + 8],
                                                 EPS_NORM, None,
                                                 OP.max, OP.bypass))
                        vss.sync(v.tensor_tensor(stat[:, sb + 9:sb + 10],
                                                 stat[:, sb + 5:sb + 6],
                                                 stat[:, sb + 8:sb + 9],
                                                 OP.mult))
                        vss.sync(v.reciprocal(stat[:, sb + 10:sb + 11],
                                              stat[:, sb + 9:sb + 10]))
                        if i >= 2:
                            v.wait_ge(s_spq[par], 16 * (i // 2))
                        v.tensor_scalar(q2f_bf[:, par * C:(par + 1) * C],
                                        q2_f[:, par * C:(par + 1) * C],
                                        stat[:, sb + 10:sb + 11],
                                        INV64, OP.mult, OP.mult).then_inc(s_fin, 1)
                        v.wait_ge(s_sqs, 4 * i + 4)
                        vss.sync(v.tensor_scalar(stat[:, sb + 14:sb + 15],
                                                 stat[:, sb + 13:sb + 14],
                                                 EPS_SUM, None,
                                                 OP.add, OP.bypass))
                        vss.sync(v.reciprocal(stat[:, sb + 15:sb + 16],
                                              stat[:, sb + 14:sb + 15]))
                        if i >= 2:
                            v.wait_ge(s_spk[par], 16 * (i // 2))
                        v.tensor_scalar(k2n_bf[:, par * C:(par + 1) * C],
                                        k2_f[:, par * C:(par + 1) * C],
                                        stat[:, sb + 15:sb + 16],
                                        None, OP.mult, OP.bypass).then_inc(s_fin, 1)
                        if i >= 2:
                            v.wait_ge(s_rcm, i - 1)
                        v.tensor_tensor(k2sq_bf[:, par * C:(par + 1) * C],
                                        k2n_bf[:, par * C:(par + 1) * C],
                                        k2n_bf[:, par * C:(par + 1) * C],
                                        OP.mult).then_inc(s_fin, 1)

                @blk.scalar
                def _(sc):
                    def stats(k):
                        pk = k % 2
                        sb = 16 * pk
                        sc.wait_ge(s_qst, 2 * k + 1)
                        if k >= 2:
                            sc.wait_ge(s_fin, 3 * (k - 1) + 1)
                        qs_ = q_f[:, pk * C:(pk + 1) * C]
                        q2s_ = q2_f[:, pk * C:(pk + 1) * C]
                        sc.activation(q2s_, qs_, AF.Square,
                                      bias=stat[:, sb + 1:sb + 2],
                                      accum_out=stat[:, sb + 2:sb + 3]
                                      ).then_inc(s_sqs, 1)
                        sc.activation(qs_, q2s_, AF.Square,
                                      accum_out=stat[:, sb + 3:sb + 4]
                                      ).then_inc(s_sqs, 1)
                        sc.activation(stat[:, sb + 4:sb + 5],
                                      stat[:, sb + 3:sb + 4], AF.Sqrt
                                      ).then_inc(s_sqs, 1)
                        sc.wait_ge(s_qst, 2 * k + 2)
                        if k >= 2:
                            sc.wait_ge(s_fin, 3 * (k - 1) + 2)
                        sc.activation(k2_f[:, pk * C:(pk + 1) * C],
                                      k_f[:, pk * C:(pk + 1) * C], AF.Square,
                                      bias=stat[:, sb + 12:sb + 13],
                                      accum_out=stat[:, sb + 13:sb + 14]
                                      ).then_inc(s_sqs, 1)

                    for i in range(NT):
                        par = i % 2
                        sc.wait_ge(s_xtr, i + 1)
                        if i >= 2:
                            sc.wait_ge(s_mmq, 6 * (i - 1))
                        sc.copy(xT_sb[:, par * C:(par + 1) * C],
                                ps_xT[:, par * C:(par + 1) * C]
                                ).then_inc(s_xev, 1)
                        if i >= 1:
                            stats(i - 1)
                    stats(NT - 1)

            with nc.Block() as blk:
                @blk.vector
                def _(v):
                    v.tensor_copy(rrow[0:1, 0:512], ps_rc0[0:1, 0:512])
                    v.tensor_copy(rrow[0:1, 512:1024], ps_rc1[0:1, 0:512])
            with nc.Block() as blk:
                @blk.scalar
                def _(sc):
                    sc.activation(rrow[0:1, :], rrow[0:1, :], AF.Sqrt)
            with nc.Block() as blk:
                @blk.vector
                def _(v):
                    v.tensor_scalar(rrow[0:1, :], rrow[0:1, :], EPS_NORM, None,
                                    OP.max, OP.bypass).then_inc(s_vs2, 1)
                    v.wait_ge(s_vs2, 1)
                    v.reciprocal(rrow[0:1, :], rrow[0:1, :])
            with nc.Block() as blk:
                @blk.tensor
                def _(t):
                    for cch in range(8):
                        t.transpose(ps_rcT[:, cch:cch + 1],
                                    rrow[0:1, cch * 128:(cch + 1) * 128],
                                    ident_f[0:1, 0:1])
            with nc.Block() as blk:
                @blk.vector
                def _(v):
                    v.tensor_copy(rcol[:, :], ps_rcT[:, :])

        with contextlib.ExitStack() as es3:
            sem3 = sem_scope(nc, es3)
            wln_sb = es3.enter_context(nc.sbuf_tensor("wln_sb", [128, 8 * C], BF16))
            bln_r = es3.enter_context(nc.sbuf_tensor("bln_r", [1, C], BF16))
            kv_sb = es3.enter_context(nc.sbuf_tensor("kv_sb", [128, 8 * C], BF16))
            s_lk = [sem3("b_lk0"), sem3("b_lk1"),
                    sem3("b_lk2"), sem3("b_lk3")]
            s_lv = [sem3("b_lv0"), sem3("b_lv1"),
                    sem3("b_lv2"), sem3("b_lv3")]
            s_mkv = sem3("b_mkv")
            s_kev = sem3("b_kev")
            s_lwl = sem3("b_lwl")
            es3b = es3.enter_context(contextlib.ExitStack())
            k2n_t = es3b.enter_context(nc.sbuf_tensor("k2n_t", [128, 4 * C], BF16))
            vh_t = es3b.enter_context(nc.sbuf_tensor("vh_t", [128, 4 * 512], BF16))
            ps_kv = es3b.enter_context(nc.psum_tensor("ps_kv", [128, 8 * 512], F32))
            with nc.Block() as blk:
                @blk.gpsimd
                def _(g):
                    for j in range(8):
                        g.dma_start(wln_sb[:, j * C:(j + 1) * C],
                                    wln_h[j * 128:(j + 1) * 128, :]
                                    ).then_inc(s_lwl, 16)
                    g.dma_start(bln_r[:, :], bln_h[:, :]).then_inc(s_lwl, 16)
                    for hh in range(2):
                        for i in range(NT):
                            gi = hh * NT + i
                            if gi >= 4:
                                g.wait_ge(s_mkv, gi - 3)
                            g.dma_start(
                                vh_t[:, (gi % 4) * 512:(gi % 4 + 1) * 512],
                                bass.AP(v_s, i * 128 * C + hh * 512,
                                        [[C, 128], [1, 512]]),
                            ).then_inc(s_lv[gi % 4], 16)
                    g.wait_ge(s_lwl, 16 * 9)

                @blk.sync
                def _(sy):
                    for hh in range(2):
                        for i in range(NT):
                            gi = hh * NT + i
                            if gi >= 4:
                                sy.wait_ge(s_mkv, gi - 3)
                            sy.dma_start(k2n_t[:, (gi % 4) * C:(gi % 4 + 1) * C],
                                         k2n_s[i * 128:(i + 1) * 128, :]
                                         ).then_inc(s_lk[gi % 4], 16)

                @blk.tensor
                def _(t):
                    for hh in range(2):
                        if hh == 1:
                            t.wait_ge(s_kev, 1)
                        for i in range(NT):
                            gi = hh * NT + i
                            t.wait_ge(s_lk[gi % 4], 16 * (gi // 4 + 1))
                            t.wait_ge(s_lv[gi % 4], 16 * (gi // 4 + 1))
                            ins = None
                            for cch in range(8):
                                ins = t.matmul(
                                    ps_kv[:, cch * 512:(cch + 1) * 512],
                                    k2n_t[:, (gi % 4) * C + cch * 128:
                                          (gi % 4) * C + (cch + 1) * 128],
                                    vh_t[:, (gi % 4) * 512:(gi % 4 + 1) * 512],
                                    start=(i == 0), stop=(i == NT - 1))
                            ins.then_inc(s_mkv, 1)

                @blk.vector
                def _(v):
                    for hh in range(2):
                        v.wait_ge(s_mkv, (hh + 1) * NT)
                        ins = None
                        for cch in range(8):
                            ins = v.tensor_scalar(
                                kv_sb[:, cch * C + hh * 512:
                                      cch * C + (hh + 1) * 512],
                                ps_kv[:, cch * 512:(cch + 1) * 512],
                                rcol[:, cch:cch + 1], None, OP.mult, OP.bypass)
                        ins.then_inc(s_kev, 1)

            es3b.close()
            with contextlib.ExitStack() as es4:
                sem4 = sem_scope(nc, es4)
                x_f2 = es4.enter_context(nc.sbuf_tensor("x_f2", [128, 2 * C], F32))
                q2f_t = es4.enter_context(nc.sbuf_tensor("q2f_t", [128, 2 * C], BF16))
                v_t2 = es4.enter_context(nc.sbuf_tensor("v_t2", [128, 2 * C], BF16))
                q2fT = es4.enter_context(nc.sbuf_tensor("q2fT", [128, 2 * C], BF16))
                vt2 = es4.enter_context(nc.sbuf_tensor("vt2", [128, 2 * C], BF16))
                vt2T = es4.enter_context(nc.sbuf_tensor("vt2T", [128, 2 * C], BF16))
                r1_f = es4.enter_context(nc.sbuf_tensor("r1_f", [128, 3 * C], F32))
                x2_f = es4.enter_context(nc.sbuf_tensor("x2_f", [128, 2 * C], F32))
                sqscr = es4.enter_context(nc.sbuf_tensor("sqscr", [128, C], F32))
                xq_i = es4.enter_context(nc.sbuf_tensor("xq_i", [128, C], I32))
                u2_f = es4.enter_context(nc.sbuf_tensor("u2_f", [128, C], F32))
                corr_i = es4.enter_context(nc.sbuf_tensor("corr_i", [128, C], I32))
                xq_bf = es4.enter_context(nc.sbuf_tensor("xq_bf", [128, 2 * C], BF16))
                xqT_ev = es4.enter_context(nc.sbuf_tensor("xqT_ev", [128, 2 * C], BF16))
                st2 = es4.enter_context(nc.sbuf_tensor("st2", [128, 24], F32))
                st3 = es4.enter_context(nc.sbuf_tensor("st3", [128, 64], F32))
                ps_tr2 = es4.enter_context(nc.psum_tensor("ps_tr2", [128, 3 * C], BF16))
                ps_t2a = es4.enter_context(nc.psum_tensor("ps_t2a", [128, 512], F32))
                ps_t2b = es4.enter_context(nc.psum_tensor("ps_t2b", [128, 512], F32))
                ps_ata = es4.enter_context(nc.psum_tensor("ps_ata", [128, 512], F32))
                ps_atb = es4.enter_context(nc.psum_tensor("ps_atb", [128, 512], F32))
                ps_dq = es4.enter_context(nc.psum_tensor("ps_dq", [1, 128], F32))
                s_lxf = [sem4("c_lxf0"),
                         sem4("c_lxf1")]
                s_lqf = [sem4("c_lqf0"),
                         sem4("c_lqf1")]
                s_lvt = [sem4("c_lvt0"),
                         sem4("c_lvt1")]
                s_trq = sem4("c_trq")
                s_teq = sem4("c_teq")
                s_trv = sem4("c_trv")
                s_tev = sem4("c_tev")
                s_trx = sem4("c_trx")
                s_tex = sem4("c_tex")
                s_mt2 = sem4("c_mt")
                s_ma = sem4("c_ma")
                s_ev2 = sem4("c_ev")
                s_sq = sem4("c_sq")
                s_var = sem4("c_var")
                s_rs = sem4("c_rs")
                s_qx = sem4("c_qx")
                s_x2v = sem4("c_x2v")
                s_dqt = sem4("c_dqt")
                s_dqe = sem4("c_dqe")
                s_cvs = sem4("c_cvs")
                s_spx2 = [sem4("c_spx0"),
                          sem4("c_spx1")]
                s_spxq = [sem4("c_spq0"),
                          sem4("c_spq1")]
                ps_t2 = [ps_t2a, ps_t2b]
                ps_at = [ps_ata, ps_atb]
                NS4 = NT + 4
                with nc.Block() as blk:
                    @blk.sync
                    def _(sy):
                        for i in range(NT):
                            par = i % 2
                            if i >= 2:
                                sy.wait_ge(s_trq, i - 1)
                                sy.wait_ge(s_ev2, 4 * i - 4)
                            sy.dma_start(x_f2[:, par * C:(par + 1) * C],
                                         x_h[i * 128:(i + 1) * 128, :]
                                         ).then_inc(s_lxf[par], 16)
                            sy.dma_start(q2f_t[:, par * C:(par + 1) * C],
                                         q2f_s[i * 128:(i + 1) * 128, :]
                                         ).then_inc(s_lqf[par], 16)
                            sy.dma_start(v_t2[:, par * C:(par + 1) * C],
                                         v_s[i * 128:(i + 1) * 128, :]
                                         ).then_inc(s_lvt[par], 16)

                    @blk.tensor
                    def _(t):
                        t.wait_ge(s_kev, 2)
                        t.wait_ge(s_lwl, 16 * 9)
                        for i in range(NS4):
                            par = i % 2
                            if i < NT:
                                t.wait_ge(s_lxf[par], 16 * (i // 2 + 1))
                                t.wait_ge(s_lqf[par], 16 * (i // 2 + 1))
                                t.wait_ge(s_lvt[par], 16 * (i // 2 + 1))
                                if i >= 1:
                                    t.wait_ge(s_teq, i)
                                ins = None
                                for b in range(8):
                                    ins = t.transpose(
                                        ps_tr2[:, b * 128:(b + 1) * 128],
                                        q2f_t[:, par * C + b * 128:
                                              par * C + (b + 1) * 128],
                                        ident_b[:, :])
                                ins.then_inc(s_trq, 1)
                                t.wait_ge(s_teq, i + 1)
                                for dh in range(2):
                                    if i >= 1:
                                        t.wait_ge(s_ev2, 4 * (i - 1) + dh + 1)
                                    ins = None
                                    for cch in range(8):
                                        ins = t.matmul(
                                            ps_t2[dh][:, 0:512],
                                            q2fT[:, par * C + cch * 128:
                                                 par * C + (cch + 1) * 128],
                                            kv_sb[:, cch * C + dh * 512:
                                                  cch * C + (dh + 1) * 512],
                                            start=(cch == 0), stop=(cch == 7))
                                    ins.then_inc(s_mt2, 1)
                                t.wait_ge(s_ev2, 4 * i + 2)
                                if i >= 1:
                                    t.wait_ge(s_tev, i)
                                ins = None
                                for b in range(8):
                                    ins = t.transpose(
                                        ps_tr2[:, C + b * 128:C + (b + 1) * 128],
                                        vt2[:, par * C + b * 128:
                                            par * C + (b + 1) * 128],
                                        ident_b[:, :])
                                ins.then_inc(s_trv, 1)
                                t.wait_ge(s_tev, i + 1)
                                for eh in range(2):
                                    if i >= 1:
                                        t.wait_ge(s_ev2, 4 * (i - 1) + eh + 3)
                                    for dch in range(8):
                                        t.matmul(ps_at[eh][:, 0:512],
                                                 vt2T[:, par * C + dch * 128:
                                                      par * C + (dch + 1) * 128],
                                                 wln_sb[:, dch * C + eh * 512:
                                                        dch * C + (eh + 1) * 512],
                                                 start=(dch == 0), stop=False)
                                    t.matmul(ps_at[eh][:, 0:512],
                                             ones_row_b[0:1, :],
                                             bln_r[0:1, eh * 512:(eh + 1) * 512],
                                             start=False,
                                             stop=True).then_inc(s_ma, 1)
                            k = i - 4
                            if 0 <= k < NT:
                                pk = k % 2
                                b3 = 8 * (k % 8)
                                t.wait_ge(s_qx, 2 * k + 1)
                                if k >= 1:
                                    t.wait_ge(s_dqe, k)
                                t.transpose(ps_dq[0:1, :], st3[:, b3 + 5:b3 + 6],
                                            ident_f[:, :]).then_inc(s_dqt, 1)
                                t.wait_ge(s_qx, 2 * k + 2)
                                if k >= 1:
                                    t.wait_ge(s_tex, k)
                                ins = None
                                for b in range(8):
                                    ins = t.transpose(
                                        ps_tr2[:, 2 * C + b * 128:
                                               2 * C + (b + 1) * 128],
                                        xq_bf[:, pk * C + b * 128:
                                              pk * C + (b + 1) * 128],
                                        ident_b[:, :])
                                ins.then_inc(s_trx, 1)

                    @blk.vector
                    def _(v):
                        vss = SelfSync(v, s_cvs)

                        def v_quant_a(k):
                            p2 = k % 2
                            p3 = k % 3
                            s2 = 6 * (k % 4)
                            b3 = 8 * (k % 8)
                            v.wait_ge(s_rs, k + 1)
                            vss.sync(v.reciprocal(st3[:, b3 + 6:b3 + 7],
                                                  st2[:, s2 + 5:s2 + 6]))
                            if k >= 2:
                                v.wait_ge(s_spx2[p2], 16 * (k // 2))
                            v.tensor_scalar(x2_f[:, p2 * C:(p2 + 1) * C],
                                            r1_f[:, p3 * C:(p3 + 1) * C],
                                            st2[:, s2 + 2:s2 + 3],
                                            st3[:, b3 + 6:b3 + 7],
                                            OP.add, OP.mult).then_inc(s_x2v, 1)
                            vss.sync(v.reduce_max(st3[:, b3 + 0:b3 + 1],
                                                  x2_f[:, p2 * C:(p2 + 1) * C],
                                                  AX.X,
                                                  apply_absolute_value=True))

                        def v_quant_b(k):
                            p2 = k % 2
                            b3 = 8 * (k % 8)
                            vss.sync(v.tensor_scalar(st3[:, b3 + 1:b3 + 2],
                                                     st3[:, b3 + 0:b3 + 1],
                                                     QCLIP, None,
                                                     OP.max, OP.bypass))
                            vss.sync(v.reciprocal(st3[:, b3 + 2:b3 + 3],
                                                  st3[:, b3 + 1:b3 + 2]))
                            vss.sync(v.tensor_scalar(st3[:, b3 + 3:b3 + 4],
                                                     st3[:, b3 + 2:b3 + 3],
                                                     127.0, None,
                                                     OP.mult, OP.bypass))
                            vss.sync(v.tensor_scalar(st3[:, b3 + 4:b3 + 5],
                                                     st3[:, b3 + 2:b3 + 3],
                                                     254.0, None,
                                                     OP.mult, OP.bypass))
                            vss.sync(v.tensor_scalar(st3[:, b3 + 5:b3 + 6],
                                                     st3[:, b3 + 1:b3 + 2],
                                                     m1_col[:, :], INV127,
                                                     OP.mult, OP.mult))
                            v.nop().then_inc(s_qx, 1)
                            v.tensor_scalar(xq_i[:, :],
                                            x2_f[:, p2 * C:(p2 + 1) * C],
                                            st3[:, b3 + 3:b3 + 4], None,
                                            OP.mult, OP.bypass)
                            v.tensor_scalar(u2_f[:, :],
                                            x2_f[:, p2 * C:(p2 + 1) * C],
                                            st3[:, b3 + 4:b3 + 5], None,
                                            OP.mult, OP.bypass)
                            v.scalar_tensor_tensor(u2_f[:, :], xq_i[:, :],
                                                   -2.0, u2_f[:, :],
                                                   OP.mult, OP.add)
                            v.scalar_tensor_tensor(corr_i[:, :], u2_f[:, :],
                                                   -1.0, xq_i[:, :],
                                                   OP.is_lt, OP.subtract)
                            if k >= 2:
                                v.wait_ge(s_trx, k - 1)
                            v.scalar_tensor_tensor(
                                xq_bf[:, p2 * C:(p2 + 1) * C],
                                u2_f[:, :], 1.0, corr_i[:, :],
                                OP.is_gt, OP.subtract).then_inc(s_qx, 1)

                        def v_stats(j):
                            s2 = 6 * (j % 4)
                            v.wait_ge(s_sq, 2 * j + 2)
                            v.tensor_scalar(st2[:, s2 + 2:s2 + 3],
                                            st2[:, s2 + 0:s2 + 1],
                                            -INV1024, None, OP.mult, OP.bypass)
                            vss.sync(v.tensor_tensor(st2[:, s2 + 3:s2 + 4],
                                                     st2[:, s2 + 0:s2 + 1],
                                                     st2[:, s2 + 0:s2 + 1],
                                                     OP.mult))
                            v.scalar_tensor_tensor(
                                st2[:, s2 + 4:s2 + 5],
                                st2[:, s2 + 3:s2 + 4], -INV1024,
                                st2[:, s2 + 1:s2 + 2], OP.mult,
                                OP.add).then_inc(s_var, 1)

                        def v_evac2(jq):
                            pq = jq % 2
                            v.wait_ge(s_dqt, jq + 1)
                            v.tensor_copy(dq_row[0:1, 1 + jq * 128:
                                                  1 + (jq + 1) * 128],
                                          ps_dq[0:1, :]).then_inc(s_dqe, 1)
                            v.wait_ge(s_trx, jq + 1)
                            if jq >= 2:
                                v.wait_ge(s_spxq[pq], 16 * (jq // 2))
                            v.tensor_copy(xqT_ev[:, pq * C:(pq + 1) * C],
                                          ps_tr2[:, 2 * C:3 * C]
                                          ).then_inc(s_tex, 1)

                        for i in range(NS4):
                            par = i % 2
                            p3 = i % 3
                            if i < NT:
                                v.wait_ge(s_trq, i + 1)
                                if i >= 2:
                                    v.wait_ge(s_mt2, 2 * (i - 2) + 2)
                                v.tensor_copy(q2fT[:, par * C:(par + 1) * C],
                                              ps_tr2[:, 0:C]).then_inc(s_teq, 1)
                            if 0 <= i - 3 < NT:
                                v_quant_a(i - 3)
                            if i < NT:
                                for dh in range(2):
                                    v.wait_ge(s_mt2, 2 * i + dh + 1)
                                    if i >= 2 and dh == 0:
                                        v.wait_ge(s_trv, i - 1)
                                    v.tensor_tensor(
                                        vt2[:, par * C + dh * 512:
                                            par * C + (dh + 1) * 512],
                                        ps_t2[dh][:, 0:512],
                                        v_t2[:, par * C + dh * 512:
                                             par * C + (dh + 1) * 512],
                                        OP.add).then_inc(s_ev2, 1)
                                v.wait_ge(s_trv, i + 1)
                                if i >= 2:
                                    v.wait_ge(s_ma, 2 * (i - 2) + 2)
                                v.tensor_copy(vt2T[:, par * C:(par + 1) * C],
                                              ps_tr2[:, C:2 * C]
                                              ).then_inc(s_tev, 1)
                            if 0 <= i - 3 < NT:
                                v_quant_b(i - 3)
                            if 0 <= i - 1 < NT:
                                v_stats(i - 1)
                            if i < NT:
                                for eh in range(2):
                                    v.wait_ge(s_ma, 2 * i + eh + 1)
                                    if i >= 3 and eh == 0:
                                        v.wait_ge(s_sq, 2 * i - 4)
                                    v.tensor_tensor(
                                        r1_f[:, p3 * C + eh * 512:
                                             p3 * C + (eh + 1) * 512],
                                        ps_at[eh][:, 0:512],
                                        x_f2[:, par * C + eh * 512:
                                             par * C + (eh + 1) * 512],
                                        OP.add).then_inc(s_ev2, 1)
                            if 0 <= i - 4 < NT:
                                v_evac2(i - 4)

                    @blk.scalar
                    def _(sc):
                        def s_rstd(j):
                            s2 = 6 * (j % 4)
                            sc.wait_ge(s_var, j + 1)
                            sc.activation(st2[:, s2 + 5:s2 + 6],
                                          st2[:, s2 + 4:s2 + 5], AF.Sqrt,
                                          bias=eps_col[:, :], scale=INV1024
                                          ).then_inc(s_rs, 1)

                        for i in range(NT):
                            p3 = i % 3
                            s2 = 6 * (i % 4)
                            if i >= 2:
                                s_rstd(i - 2)
                            sc.wait_ge(s_ev2, 4 * i + 4)
                            sc.activation(sqscr[:, :],
                                          r1_f[:, p3 * C:(p3 + 1) * C],
                                          AF.Identity,
                                          accum_out=st2[:, s2 + 0:s2 + 1]
                                          ).then_inc(s_sq, 1)
                            sc.activation(sqscr[:, :],
                                          r1_f[:, p3 * C:(p3 + 1) * C],
                                          AF.Square,
                                          accum_out=st2[:, s2 + 1:s2 + 2]
                                          ).then_inc(s_sq, 1)
                        s_rstd(NT - 2)
                        s_rstd(NT - 1)

                    @blk.gpsimd
                    def _(g):
                        def x2_spill(k):
                            g.wait_ge(s_x2v, k + 1)
                            g.dma_start(x2_s[k * 128:(k + 1) * 128, :],
                                        x2_f[:, (k % 2) * C:(k % 2 + 1) * C]
                                        ).then_inc(s_spx2[k % 2], 16)

                        def xq_spill(k):
                            pk = k % 2
                            g.wait_ge(s_tex, k + 1)
                            g.dma_start(
                                bass.AP(xqT_s, 1 + k * 128,
                                        [[N + 2, 128], [128 * (N + 2), 8],
                                         [1, 128]]),
                                bass.AP(xqT_ev, pk * C,
                                        [[2 * C, 128], [128, 8], [1, 128]]),
                            ).then_inc(s_spxq[pk], 16)

                        for k in range(NT):
                            x2_spill(k)
                            xq_spill(k)
                        for sm in (s_spx2, s_spxq):
                            g.wait_ge(sm[0], 16 * (NT // 2))
                            g.wait_ge(sm[1], 16 * (NT // 2))


        with contextlib.ExitStack() as es5:
            sem5 = sem_scope(nc, es5)
            slab = es5.enter_context(nc.sbuf_tensor("slab", [128, 2 * 4096], BF16))
            xqT_t = es5.enter_context(nc.sbuf_tensor("xqT_t", [128, 2 * 8 * 258], BF16))
            x2_t = es5.enter_context(nc.sbuf_tensor("x2_t", [128, 2 * 2048], F32))
            dqB = es5.enter_context(nc.sbuf_tensor("dqB", [128, 258], F32))
            hpre = es5.enter_context(nc.sbuf_tensor("hpre", [128, 3 * 258], BF16))
            acc3 = es5.enter_context(nc.sbuf_tensor("acc3", [128, 4 * 256], BF16))
            hs = es5.enter_context(nc.sbuf_tensor("hs", [128, 2 * 32 * 256], BF16))
            runmax = es5.enter_context(nc.sbuf_tensor("runmax", [128, 2 * 1024], BF16))
            runminB = es5.enter_context(nc.sbuf_tensor("runminB", [128, 2 * 1024], BF16))
            absB = es5.enter_context(nc.sbuf_tensor("absB", [128, 2 * 256], F32))
            u2b = es5.enter_context(nc.sbuf_tensor("u2b", [128, 1024], F32))
            xib = es5.enter_context(nc.sbuf_tensor("xib", [128, 1024], I32))
            corrb = es5.enter_context(nc.sbuf_tensor("corrb", [128, 1024], BF16))
            hq = es5.enter_context(nc.sbuf_tensor("hq", [128, 32 * 256], BF16))
            M2c = es5.enter_context(nc.sbuf_tensor("M2c", [128, 4], F32))
            Mc2 = es5.enter_context(nc.sbuf_tensor("Mc2", [128, 4], F32))
            dqoC = es5.enter_context(nc.sbuf_tensor("dqoC", [128, 4], F32))
            mrow = es5.enter_context(nc.sbuf_tensor("mrow", [1, 2 * 256], F32))
            s2row = es5.enter_context(nc.sbuf_tensor("s2row", [1, 2 * 256], F32))
            out_t = es5.enter_context(nc.sbuf_tensor("out_t", [128, 2048], F32))
            st4 = es5.enter_context(nc.sbuf_tensor("st4", [128, 32], F32))
            ps_h0 = es5.enter_context(nc.psum_tensor("ps_h0", [128, 512], F32))
            ps_h1 = es5.enter_context(nc.psum_tensor("ps_h1", [128, 512], F32))
            ps_o0 = es5.enter_context(nc.psum_tensor("ps_o0", [128, 512], F32))
            ps_o1 = es5.enter_context(nc.psum_tensor("ps_o1", [128, 512], F32))
            ps_o2 = es5.enter_context(nc.psum_tensor("ps_o2", [128, 512], F32))
            ps_o3 = es5.enter_context(nc.psum_tensor("ps_o3", [128, 512], F32))
            ps_mi = es5.enter_context(nc.psum_tensor("ps_mi", [128, 512], F32))
            ps_s2 = es5.enter_context(nc.psum_tensor("ps_s2", [128, 256], F32))
            s_lxq = sem5("d_lxq")
            s_lx2 = sem5("d_lx2")
            s_lsl = [sem5("d_lsl0"),
                     sem5("d_lsl1")]
            s_dqb = sem5("d_dqb")
            s_dqbe = sem5("d_dqbe")
            s_m1 = sem5("d_m1")
            s_hev = sem5("d_hev")
            s_c1 = sem5("d_c1")
            s_c3 = sem5("d_c3")
            s_sil = sem5("d_sil")
            s_rmx = sem5("d_rmx")
            s_rmn = sem5("d_rmn")
            s_s2a = sem5("d_s2a")
            s_str = sem5("d_str")
            s_mc = sem5("d_mc")
            s_mtr = sem5("d_mtr")
            s_s2r = sem5("d_s2r")
            s_msf = sem5("d_msf")
            s_s2b = sem5("d_s2b")
            s_uxi = sem5("d_uxi")
            s_xi = sem5("d_xi")
            s_hq = sem5("d_hq")
            s_m2 = sem5("d_m2")
            s_oev = sem5("d_oev")
            s_lnv = sem5("d_lnv")
            s_ln3 = sem5("d_ln3")
            s_sq3 = sem5("d_sq3")
            s_sqr3 = sem5("d_sqr3")
            s_nrm = sem5("d_nrm")
            s_va = sem5("d_va")
            s_vb = sem5("d_vb")
            s_ost = [sem5("d_ost0"),
                     sem5("d_ost1")]
            s_dvs = sem5("d_dvs")
            ps_mm1 = [ps_h0, ps_h1]
            ps_mm2 = [ps_o0, ps_o1, ps_o2, ps_o3]
            NB = 16
            with nc.Block() as blk:
                @blk.sync
                def _(sy):
                    for i in range(NT3):
                        par = i % 2
                        if i >= 2:
                            sy.wait_ge(s_m1, 32 * (i - 1))
                        sy.dma_start(
                            bass.AP(xqT_t, par * 8 * 258,
                                    [[2 * 8 * 258, 128], [258, 8], [1, 258]]),
                            bass.AP(xqT_s, i * 256,
                                    [[N + 2, 128], [128 * (N + 2), 8], [1, 258]]),
                        ).then_inc(s_lxq, 16)
                        for s8 in range(8):
                            gsl = 8 * i + s8
                            if gsl >= 2:
                                sy.wait_ge(s_m1, 4 * (gsl - 1))
                            sy.dma_start(
                                bass.AP(slab, (gsl % 2) * 4096,
                                        [[2 * 4096, 128], [512, 8], [1, 512]]),
                                bass.AP(t1T_s, s8 * 512,
                                        [[H, 128], [128 * H, 8], [1, 512]]),
                            ).then_inc(s_lsl[gsl % 2], 16)
                        if i >= 2:
                            sy.wait_ge(s_nrm, 2 * i - 2)
                        sy.dma_start(
                            bass.AP(x2_t, (i % 2) * 2048,
                                    [[2 * 2048, 128], [1024, 2], [1, 1024]]),
                            bass.AP(x2_s, i * 256 * C,
                                    [[C, 128], [128 * C, 2], [1, 1024]]),
                        ).then_inc(s_lx2, 16)

                @blk.tensor
                def _(t):
                    def mm1_slabs(i, s8lo, s8hi):
                        for s8 in range(s8lo, s8hi):
                            gsl = 8 * i + s8
                            t.wait_ge(s_lsl[gsl % 2], 16 * (gsl // 2 + 1))
                            for cc in range(4):
                                hc = 4 * s8 + cc
                                if 32 * i + hc >= 2:
                                    t.wait_ge(s_hev, 32 * i + hc - 1)
                                ins = None
                                for j in range(8):
                                    ins = t.matmul(
                                        ps_mm1[hc % 2][:, 0:258],
                                        slab[:, (gsl % 2) * 4096 +
                                             j * 512 + cc * 128:
                                             (gsl % 2) * 4096 +
                                             j * 512 + (cc + 1) * 128],
                                        xqT_t[:, (i % 2) * 8 * 258 + j * 258:
                                              (i % 2) * 8 * 258 +
                                              (j + 1) * 258],
                                        start=(j == 0), stop=(j == 7))
                                ins.then_inc(s_m1, 1)

                    for i in range(NT3 + 1):
                        if i < NT3:
                            if i >= 1:
                                t.wait_ge(s_dqbe, i)
                            if i >= 2:
                                t.wait_ge(s_mc, i - 1)
                                t.wait_ge(s_msf, i - 1)
                            t.matmul(ps_mi[:, 0:258], ones_row_f[0:1, :],
                                     dq_row[0:1, i * 256:i * 256 + 258],
                                     start=True, stop=True).then_inc(s_dqb, 1)
                        if i >= 1:
                            jj = i - 1
                            pj = jj % 2
                            t.wait_ge(s_s2a, jj + 1)
                            t.wait_ge(s_dqbe, min(jj + 2, NT3))
                            ins = None
                            for k in range(2):
                                ins = t.transpose(
                                    ps_mi[:, k * 128:(k + 1) * 128],
                                    absB[:, pj * 256 + k * 128:
                                         pj * 256 + (k + 1) * 128],
                                    ident_f[:, :])
                            ins.then_inc(s_str, 1)
                            t.wait_ge(s_mc, jj + 1)
                            ins = None
                            for k in range(2):
                                ins = t.transpose(
                                    ps_mi[0:1, 256 + k * 128:256 + (k + 1) * 128],
                                    Mc2[:, 2 * pj + k:2 * pj + k + 1],
                                    ident_f[:, :])
                            ins.then_inc(s_mtr, 1)
                            t.wait_ge(s_s2r, jj + 1)
                            if jj >= 1:
                                t.wait_ge(s_uxi, NB * jj)
                            t.matmul(ps_s2[:, 0:256], ones_row_f[0:1, :],
                                     s2row[0:1, pj * 256:(pj + 1) * 256],
                                     start=True, stop=True).then_inc(s_s2b, 1)
                            if jj >= 1:
                                t.wait_ge(s_oev, 2 * jj)
                            for hc in range(32):
                                t.wait_ge(s_hq, 8 * jj + hc // 4 + 1)
                                ins = None
                                for sub in range(2):
                                    for eh in range(2):
                                        ins = t.matmul(
                                            ps_mm2[2 * sub + eh][:, 0:512],
                                            hq[:, hc * 256 + sub * 128:
                                               hc * 256 + (sub + 1) * 128],
                                            t2w_sb[:, hc * 1024 + eh * 512:
                                                   hc * 1024 + (eh + 1) * 512],
                                            start=(hc == 0), stop=(hc == 31))
                                if hc == 31:
                                    ins.then_inc(s_m2, 1)
                        if i < NT3:
                            t.wait_ge(s_lxq, 16 * (i + 1))
                            mm1_slabs(i, 0, 8)

                @blk.vector
                def _(v):
                    vss = SelfSync(v, s_dvs)
                    for i in range(NT3 + 1):
                        if i < NT3:
                            par = i % 2
                            v.wait_ge(s_dqb, i + 1)
                            v.tensor_copy(dqB[:, :], ps_mi[:, 0:258]
                                          ).then_inc(s_dqbe, 1)
                        if i >= 2:
                            jb = i - 2
                            s4b = 16 * (jb % 2)
                            for sub in range(2):
                                s4 = s4b + 8 * sub
                                v.wait_ge(s_sqr3, 2 * jb + sub + 1)
                                vss.sync(v.reciprocal(st4[:, s4 + 6:s4 + 7],
                                                      st4[:, s4 + 5:s4 + 6]))
                                if jb >= 2:
                                    v.wait_ge(s_nrm, 2 * jb - 2)
                                vss.sync(v.scalar_tensor_tensor(
                                    st4[:, s4 + 7:s4 + 8],
                                    st4[:, s4 + 1:s4 + 2], -1.0,
                                    st4[:, s4 + 6:s4 + 7], OP.mult, OP.mult))
                                v.nop().then_inc(s_vb, 1)
                        if i >= 1:
                            jj = i - 1
                            pj = jj % 2
                            v.wait_ge(s_str, jj + 1)
                            for k in range(2):
                                vss.sync(v.reduce_max(
                                    M2c[:, 2 * pj + k:2 * pj + k + 1],
                                    ps_mi[:, k * 128:(k + 1) * 128], AX.X))
                            vss.sync(v.tensor_scalar(
                                Mc2[:, 2 * pj:2 * pj + 2],
                                M2c[:, 2 * pj:2 * pj + 2], QCLIP, None,
                                OP.max, OP.bypass))
                            vss.sync(v.tensor_scalar(
                                dqoC[:, 2 * pj:2 * pj + 2],
                                Mc2[:, 2 * pj:2 * pj + 2], m2_col[:, :],
                                INV127, OP.mult, OP.mult))
                            v.nop().then_inc(s_mc, 1)
                            v.wait_ge(s_mtr, jj + 1)
                            v.tensor_copy(mrow[0:1, pj * 256:(pj + 1) * 256],
                                          ps_mi[0:1, 256:512]
                                          ).then_inc(s_msf, 1)
                            vss.sync(v.reciprocal(
                                s2row[0:1, pj * 256:(pj + 1) * 256],
                                mrow[0:1, pj * 256:(pj + 1) * 256]))
                            v.tensor_scalar(s2row[0:1, pj * 256:(pj + 1) * 256],
                                            s2row[0:1, pj * 256:(pj + 1) * 256],
                                            254.0, None, OP.mult, OP.bypass
                                            ).then_inc(s_s2r, 1)
                            v.wait_ge(s_s2b, jj + 1)
                            if jj >= 1:
                                v.wait_ge(s_m2, jj)
                            for bq in range(NB + 1):
                                if bq >= 2 and bq % 2 == 0:
                                    bp = bq - 2
                                    u2s = u2b[:, 0:1024]
                                    xis = xib[:, 0:1024]
                                    v.wait_ge(s_xi, NB * jj + bq)
                                    v.scalar_tensor_tensor(u2s, xis,
                                                           -2.0, u2s,
                                                           OP.mult, OP.add)
                                    v.scalar_tensor_tensor(corrb[:, :], u2s,
                                                           -1.0, xis,
                                                           OP.is_lt, OP.subtract)
                                    v.scalar_tensor_tensor(
                                        hq[:, bp * 512:(bp + 2) * 512],
                                        u2s, 1.0, corrb[:, :],
                                        OP.is_gt, OP.subtract).then_inc(s_hq, 1)
                                if bq < NB:
                                    bs = (bq % 2) * 512
                                    if jj >= 1 or bq >= 2:
                                        v.wait_ge(s_xi, NB * jj + bq - 1)
                                    ins = None
                                    for cc in range(2):
                                        hc = 2 * bq + cc
                                        hsl = hs[:, pj * 8192 + hc * 256:
                                                 pj * 8192 + (hc + 1) * 256]
                                        ins = v.tensor_tensor(
                                            u2b[:, bs + cc * 256:
                                                bs + (cc + 1) * 256],
                                            hsl, ps_s2[:, 0:256], OP.mult)
                                    ins.then_inc(s_uxi, 1)
                            v.wait_ge(s_m2, jj + 1)
                            v.wait_ge(s_lx2, 16 * (jj + 1))
                            xb = (jj % 2) * 2048
                            for sub in range(2):
                                if jj >= 2:
                                    v.wait_ge(s_nrm, 2 * (jj - 2) + sub + 1)
                                ins = None
                                for eh in range(2):
                                    ins = v.scalar_tensor_tensor(
                                        x2_t[:, xb + sub * 1024 + eh * 512:
                                             xb + sub * 1024 + (eh + 1) * 512],
                                        ps_mm2[2 * sub + eh][:, 0:512],
                                        dqoC[:, 2 * pj + sub:2 * pj + sub + 1],
                                        x2_t[:, xb + sub * 1024 + eh * 512:
                                             xb + sub * 1024 + (eh + 1) * 512],
                                        OP.mult, OP.add)
                                ins.then_inc(s_oev, 1)

                        if i < NT3:
                            par = i % 2
                            for hcs in range(33):
                                if hcs < 32:
                                    hc = hcs
                                    v.wait_ge(s_m1, 32 * i + hc + 1)
                                    v.tensor_tensor(hpre[:, (hc % 3) * 258:
                                                         (hc % 3 + 1) * 258],
                                                    ps_mm1[hc % 2][:, 0:258],
                                                    dqB[:, :], OP.mult
                                                    ).then_inc(s_hev, 1)
                                if hcs >= 1:
                                    hc = hcs - 1
                                    g = 32 * i + hc
                                    v.wait_ge(s_c1, g + 1)
                                    a3 = acc3[:, (hc % 4) * 256:
                                              (hc % 4 + 1) * 256]
                                    v.scalar_tensor_tensor(
                                        a3,
                                        hpre[:, (hc % 3) * 258:
                                             (hc % 3) * 258 + 256],
                                        cw_sb[:, 3 * hc:3 * hc + 1],
                                        a3, OP.mult, OP.add)
                                    v.scalar_tensor_tensor(
                                        a3,
                                        hpre[:, (hc % 3) * 258 + 2:
                                             (hc % 3) * 258 + 258],
                                        cw_sb[:, 3 * hc + 2:3 * hc + 3],
                                        a3, OP.mult, OP.add).then_inc(s_c3, 1)
                            for bq4 in range(8):
                                v.wait_ge(s_sil, 32 * i + 4 * bq4 + 4)
                                hb = hs[:, par * 8192 + bq4 * 1024:
                                        par * 8192 + (bq4 + 1) * 1024]
                                rx = runmax[:, par * 1024:(par + 1) * 1024]
                                rn = runminB[:, par * 1024:(par + 1) * 1024]
                                if bq4 == 0:
                                    v.tensor_copy(rx, hb).then_inc(s_rmx, 1)
                                    v.tensor_copy(rn, hb).then_inc(s_rmn, 1)
                                else:
                                    v.tensor_tensor(rx, rx, hb, OP.max
                                                    ).then_inc(s_rmx, 1)
                                    v.tensor_tensor(rn, rn, hb, OP.min
                                                    ).then_inc(s_rmn, 1)
                            if i >= 2:
                                v.wait_ge(s_str, i - 1)
                            rb = runminB[:, par * 1024:(par + 1) * 1024]
                            rx = runmax[:, par * 1024:(par + 1) * 1024]
                            ab = absB[:, par * 256:(par + 1) * 256]
                            v.tensor_tensor(ab, rb[:, 0:256], rb[:, 256:512],
                                            OP.min)
                            v.tensor_tensor(ab, ab, rb[:, 512:768], OP.min)
                            v.tensor_tensor(ab, ab, rb[:, 768:1024], OP.min)
                            v.tensor_scalar(ab, ab, -1.0, None,
                                            OP.mult, OP.bypass)
                            v.tensor_tensor(ab, ab, rx[:, 0:256], OP.max)
                            v.tensor_tensor(ab, ab, rx[:, 256:512], OP.max)
                            v.tensor_tensor(ab, ab, rx[:, 512:768], OP.max)
                            v.tensor_tensor(ab, ab, rx[:, 768:1024],
                                            OP.max).then_inc(s_s2a, 1)
                        if i >= 1:
                            ja = i - 1
                            xb2 = (ja % 2) * 2048
                            s4b = 16 * (ja % 2)
                            for sub in range(2):
                                s4 = s4b + 8 * sub
                                v.wait_ge(s_sq3, 2 * ja + sub + 1)
                                vss.sync(v.tensor_scalar(
                                    st4[:, s4 + 1:s4 + 2],
                                    st4[:, s4 + 0:s4 + 1], INV1024, None,
                                    OP.mult, OP.bypass))
                                vss.sync(v.tensor_tensor(
                                    st4[:, s4 + 3:s4 + 4],
                                    st4[:, s4 + 1:s4 + 2],
                                    st4[:, s4 + 1:s4 + 2], OP.mult))
                                vss.sync(v.scalar_tensor_tensor(
                                    st4[:, s4 + 4:s4 + 5],
                                    st4[:, s4 + 2:s4 + 3], INV1024,
                                    st4[:, s4 + 3:s4 + 4],
                                    OP.mult, OP.subtract))
                                vss.sync(v.tensor_scalar(
                                    st4[:, s4 + 4:s4 + 5],
                                    st4[:, s4 + 4:s4 + 5], LN_EPS, None,
                                    OP.add, OP.bypass))
                                v.nop().then_inc(s_va, 1)
                    jb = NT3 - 1
                    s4b = 16 * (jb % 2)
                    for sub in range(2):
                        s4 = s4b + 8 * sub
                        v.wait_ge(s_sqr3, 2 * jb + sub + 1)
                        vss.sync(v.reciprocal(st4[:, s4 + 6:s4 + 7],
                                              st4[:, s4 + 5:s4 + 6]))
                        vss.sync(v.scalar_tensor_tensor(
                            st4[:, s4 + 7:s4 + 8],
                            st4[:, s4 + 1:s4 + 2], -1.0,
                            st4[:, s4 + 6:s4 + 7], OP.mult, OP.mult))
                        v.nop().then_inc(s_vb, 1)

                @blk.scalar
                def _(sc):
                    def s_sqrt3(js):
                        s4b = 16 * (js % 2)
                        for sub in range(2):
                            s4 = s4b + 8 * sub
                            sc.wait_ge(s_va, 2 * js + sub + 1)
                            sc.activation(st4[:, s4 + 5:s4 + 6],
                                          st4[:, s4 + 4:s4 + 5], AF.Sqrt
                                          ).then_inc(s_sqr3, 1)

                    def s_norm3(jn):
                        xb = (jn % 2) * 2048
                        s4b = 16 * (jn % 2)
                        for sub in range(2):
                            s4 = s4b + 8 * sub
                            sc.wait_ge(s_vb, 2 * jn + sub + 1)
                            if jn >= 1:
                                sc.wait_ge(s_ost[sub], 16 * jn)
                            sc.activation(
                                out_t[:, sub * 1024:(sub + 1) * 1024],
                                x2_t[:, xb + sub * 1024:
                                     xb + (sub + 1) * 1024],
                                AF.Identity,
                                bias=st4[:, s4 + 7:s4 + 8],
                                scale=st4[:, s4 + 6:s4 + 7]
                                ).then_inc(s_nrm, 1)
                            sc.wait_ge(s_nrm, 2 * jn + sub + 1)
                            sc.dma_start(
                                out_h[jn * 256 + sub * 128:
                                      jn * 256 + (sub + 1) * 128, :],
                                out_t[:, sub * 1024:(sub + 1) * 1024]
                            ).then_inc(s_ost[sub], 16)

                    for i in range(NT3 + 1):
                        if i >= 2:
                            s_sqrt3(i - 2)
                        if i >= 1:
                            jj = i - 1
                            for bq in range(NB):
                                bs = (bq % 2) * 512
                                sc.wait_ge(s_uxi, NB * jj + bq + 1)
                                sc.activation(xib[:, bs:bs + 512],
                                              u2b[:, bs:bs + 512],
                                              AF.Identity, scale=0.5
                                              ).then_inc(s_xi, 1)
                            xb = (jj % 2) * 2048
                            s4b = 16 * (jj % 2)
                            for sub in range(2):
                                s4 = s4b + 8 * sub
                                sc.wait_ge(s_oev, 2 * jj + sub + 1)
                                if jj >= 2:
                                    sc.wait_ge(s_ost[sub], 16 * (jj - 1))
                                sc.activation(
                                    out_t[:, sub * 1024:(sub + 1) * 1024],
                                    x2_t[:, xb + sub * 1024:
                                         xb + (sub + 1) * 1024],
                                    AF.Identity,
                                    accum_out=st4[:, s4 + 0:s4 + 1])
                                sc.activation(
                                    out_t[:, sub * 1024:(sub + 1) * 1024],
                                    x2_t[:, xb + sub * 1024:
                                         xb + (sub + 1) * 1024],
                                    AF.Square,
                                    accum_out=st4[:, s4 + 2:s4 + 3]
                                    ).then_inc(s_sq3, 1)
                        if i < NT3:
                            par = i % 2
                            for hcs in range(33):
                                if hcs < 32:
                                    hc = hcs
                                    g = 32 * i + hc
                                    sc.wait_ge(s_hev, g + 1)
                                    if g >= 4:
                                        sc.wait_ge(s_sil, g - 3)
                                    sc.activation(
                                        acc3[:, (hc % 4) * 256:
                                             (hc % 4 + 1) * 256],
                                        hpre[:, (hc % 3) * 258 + 1:
                                             (hc % 3) * 258 + 257],
                                        AF.Identity,
                                        scale=cw_sb[:, 3 * hc + 1:3 * hc + 2],
                                        bias=cb_sb[:, hc:hc + 1]
                                        ).then_inc(s_c1, 1)
                                if hcs >= 1:
                                    hc = hcs - 1
                                    sc.wait_ge(s_c3, 32 * i + hc + 1)
                                    if hc == 0 and i >= 2:
                                        sc.wait_ge(s_hq, 8 * (i - 1))
                                        sc.wait_ge(s_rmx, 8 * (i - 1))
                                        sc.wait_ge(s_rmn, 8 * (i - 1))
                                    sc.activation(
                                        hs[:, par * 8192 + hc * 256:
                                           par * 8192 + (hc + 1) * 256],
                                        acc3[:, (hc % 4) * 256:
                                             (hc % 4 + 1) * 256],
                                        AF.Silu).then_inc(s_sil, 1)
                        if i >= 2:
                            s_norm3(i - 2)
                    s_sqrt3(NT3 - 1)
                    s_norm3(NT3 - 1)
                    sc.wait_ge(s_ost[0], 16 * NT3)
                    sc.wait_ge(s_ost[1], 16 * NT3)

    return nc


_NC_CACHE = {}


def kernel(L0_attn_x, L0_attn_Wqkv, L0_attn_bqkv, L0_attn_Wln, L0_attn_bln,
           L0_ffn_w1, L0_ffn_conv_w, L0_ffn_conv_b, L0_ffn_w2):
    B = L0_attn_x.shape[0]
    x = np.ascontiguousarray(np.asarray(L0_attn_x, dtype=np.float32))
    wqkv = np.ascontiguousarray(np.asarray(L0_attn_Wqkv, dtype=np.float32))
    bqkv = np.ascontiguousarray(np.asarray(L0_attn_bqkv,
                                           dtype=np.float32).reshape(1, M3))
    wln = np.ascontiguousarray(np.asarray(L0_attn_Wln, dtype=np.float32))
    bln = np.ascontiguousarray(np.asarray(L0_attn_bln,
                                          dtype=np.float32).reshape(1, C))
    w1 = np.ascontiguousarray(np.asarray(L0_ffn_w1, dtype=np.float32))
    cw = np.ascontiguousarray(
        np.asarray(L0_ffn_conv_w, dtype=np.float32).reshape(32, 128, 3)
        .transpose(1, 0, 2).reshape(128, 96))
    cb = np.ascontiguousarray(
        np.asarray(L0_ffn_conv_b, dtype=np.float32).reshape(32, 128).T)
    w2 = np.ascontiguousarray(np.asarray(L0_ffn_w2, dtype=np.float32))

    if "nc" not in _NC_CACHE:
        _NC_CACHE["nc"] = build_nc()
    nc = _NC_CACHE["nc"]

    in_maps = []
    for b in range(B):
        in_maps.append({
            "x": np.ascontiguousarray(x[b]),
            "wqkv": wqkv, "bqkv": bqkv, "wln": wln, "bln": bln,
            "w1": w1, "cw": cw, "cb": cb, "w2": w2,
        })
    res = run_bass_kernel_spmd(nc, in_maps, core_ids=list(range(B)))
    out = np.stack([np.asarray(r["out"]) for r in res.results], axis=0)
    return out


# version 27
# speedup vs baseline: 1.0363x; 1.0363x over previous
import contextlib
import numpy as np

import concourse.bass as bass
import concourse.mybir as mybir
from concourse.bass_utils import run_bass_kernel_spmd

F32 = mybir.dt.float32
BF16 = mybir.dt.bfloat16
I32 = mybir.dt.int32
AF = mybir.ActivationFunctionType
OP = mybir.AluOpType
AX = mybir.AxisListType

N = 4096
C = 1024
M3 = 3 * C
H = 4096
NT = N // 128
T3 = 256
NT3 = N // T3
EPS_SUM = 1e-7
EPS_NORM = 1e-12
LN_EPS = 1e-5
QCLIP = 1e-5
INV127 = 1.0 / 127.0
INV64 = 1.0 / 64.0
INV1024 = 1.0 / 1024.0


_DEBUG_SCRATCH = False


def sem_scope(nc, es):
    handles = []

    def closer():
        nc.clear_and_free_semaphores(handles)
        nc.all_engine_barrier()

    es.callback(closer)

    def alloc(name):
        h = nc.alloc_semaphore(name=name)
        handles.append(h)
        return h

    return alloc


class SelfSync:

    def __init__(self, eng, sem):
        self.eng, self.sem, self.n = eng, sem, 0

    def sync(self, ins):
        self.n += 1
        ins.then_inc(self.sem, 1)
        self.eng.wait_ge(self.sem, self.n)


def build_nc():
    nc = bass.Bass("TRN2", target_bir_lowering=False,
                   detect_race_conditions=False)

    x_h = nc.dram_tensor("x", [N, C], F32, kind="ExternalInput")
    wqkv_h = nc.dram_tensor("wqkv", [C, M3], F32, kind="ExternalInput")
    bqkv_h = nc.dram_tensor("bqkv", [1, M3], F32, kind="ExternalInput")
    wln_h = nc.dram_tensor("wln", [C, C], F32, kind="ExternalInput")
    bln_h = nc.dram_tensor("bln", [1, C], F32, kind="ExternalInput")
    w1_h = nc.dram_tensor("w1", [H, C], F32, kind="ExternalInput")
    cw_h = nc.dram_tensor("cw", [128, 96], F32, kind="ExternalInput")
    cb_h = nc.dram_tensor("cb", [128, 32], F32, kind="ExternalInput")
    w2_h = nc.dram_tensor("w2", [C, H], F32, kind="ExternalInput")
    out_h = nc.dram_tensor("out", [N, C], F32, kind="ExternalOutput")

    _skind = "Internal" if not _DEBUG_SCRATCH else "ExternalOutput"
    q2f_s = nc.dram_tensor("q2f_s", [N, C], BF16, kind=_skind)
    k2n_s = nc.dram_tensor("k2n_s", [N, C], BF16, kind=_skind)
    v_s = nc.dram_tensor("v_s", [N, C], BF16, kind=_skind)
    x2_s = nc.dram_tensor("x2_s", [N, C], F32, kind=_skind)
    xqT_s = nc.dram_tensor("xqT_s", [C, N + 2], BF16, kind=_skind)
    t1T_s = nc.dram_tensor("t1T_s", [C, H], BF16, kind=_skind)

    with contextlib.ExitStack() as es1:
        sem1 = sem_scope(nc, es1)
        ident_b = es1.enter_context(nc.sbuf_tensor("ident_b", [128, 128], BF16))
        ident_f = es1.enter_context(nc.sbuf_tensor("ident_f", [128, 128], F32))
        ones_col_b = es1.enter_context(nc.sbuf_tensor("ones_col_b", [128, 1], BF16))
        ones_col_f = es1.enter_context(nc.sbuf_tensor("ones_col_f", [128, 1], F32))
        ones_row_b = es1.enter_context(nc.sbuf_tensor("ones_row_b", [1, 128], BF16))
        ones_row_f = es1.enter_context(nc.sbuf_tensor("ones_row_f", [1, 128], F32))
        m1_col = es1.enter_context(nc.sbuf_tensor("m1_col", [128, 1], F32))
        eps_col = es1.enter_context(nc.sbuf_tensor("eps_col", [128, 1], F32))
        m2_col = es1.enter_context(nc.sbuf_tensor("m2_col", [128, 1], F32))
        dq_row = es1.enter_context(nc.sbuf_tensor("dq_row", [1, N + 2], F32))
        rcol = es1.enter_context(nc.sbuf_tensor("rcol", [128, 8], F32))
        t2w_sb = es1.enter_context(nc.sbuf_tensor("t2w_sb", [128, 32 * 1024], BF16))
        cw_sb = es1.enter_context(nc.sbuf_tensor("cw_sb", [128, 96], F32))
        cb_sb = es1.enter_context(nc.sbuf_tensor("cb_sb", [128, 32], F32))
        zero8_b = es1.enter_context(nc.sbuf_tensor("zero8_b", [128, 8], BF16))
        s_p0 = sem1("s_p0")
        s_p0g = sem1("s_p0g")
        s_p0s = sem1("s_p0s")
        with nc.Block() as blk:
            @blk.gpsimd
            def _(g):
                g.memset(ident_b[:, :], 0.0).then_inc(s_p0, 1)
                g.wait_ge(s_p0, 1)
                g.affine_select(out=ident_b[:, :], in_=ident_b[:, :],
                                compare_op=OP.not_equal, fill=1.0,
                                base=0, pattern=[[-1, 128]], channel_multiplier=1)
                g.memset(ident_f[:, :], 0.0).then_inc(s_p0, 1)
                g.wait_ge(s_p0, 2)
                g.affine_select(out=ident_f[:, :], in_=ident_f[:, :],
                                compare_op=OP.not_equal, fill=1.0,
                                base=0, pattern=[[-1, 128]], channel_multiplier=1)
                g.memset(ones_col_b[:, :], 1.0)
                g.memset(ones_col_f[:, :], 1.0)
                g.memset(ones_row_b[:, :], 1.0)
                g.memset(ones_row_f[:, :], 1.0)
                g.memset(zero8_b[:, :], 0.0).then_inc(s_p0, 1)
                g.memset(dq_row[:, :], 0.0)
                g.memset(eps_col[:, :], LN_EPS)
                g.wait_ge(s_p0, 3)
                g.dma_start(
                    bass.AP(xqT_s, 0, [[N + 2, 128], [128 * (N + 2), 8], [1, 1]]),
                    bass.AP(zero8_b, 0, [[8, 128], [1, 8], [1, 1]]),
                    allow_slow_non_contiguous=True,
                ).then_inc(s_p0g, 16)
                g.dma_start(
                    bass.AP(xqT_s, N + 1, [[N + 2, 128], [128 * (N + 2), 8], [1, 1]]),
                    bass.AP(zero8_b, 0, [[8, 128], [1, 8], [1, 1]]),
                    allow_slow_non_contiguous=True,
                ).then_inc(s_p0g, 16)
                g.wait_ge(s_p0g, 32)

            @blk.sync
            def _(sy):
                sy.dma_start(cw_sb[:, :], cw_h[:, :]).then_inc(s_p0s, 16)
                sy.dma_start(cb_sb[:, :], cb_h[:, :]).then_inc(s_p0s, 16)
                sy.wait_ge(s_p0s, 32)

        def emit_wquant_jobs(jobs):
            stack = []
            semh = []
            def mk(f, *a):
                h = f(*a); r = h.__enter__(); stack.append(h); return r
            def mks(name):
                h = nc.alloc_semaphore(name=name)
                semh.append(h)
                return h
            for jb in jobs:
                tag, kd = jb["tag"], jb["kd"]
                jb["nb"] = kd // 128
                jb["wf"] = mk(nc.sbuf_tensor, f"{tag}_wf", [128, 3 * kd], F32)
                jb["a1m"] = mk(nc.sbuf_tensor, f"{tag}_a1m", [128, kd], F32)
                jb["tern"] = mk(nc.sbuf_tensor, f"{tag}_tern",
                                [128, 2 * kd], BF16)
                jb["csum"] = mk(nc.sbuf_tensor, f"{tag}_csum", [128, 4], F32)
                jb["acc"] = mk(nc.sbuf_tensor, f"{tag}_acc", [128, 1], F32)
                jb["msb"] = mk(nc.sbuf_tensor, f"{tag}_msb", [1, 1], F32)
                jb["thr"] = mk(nc.sbuf_tensor, f"{tag}_thr", [128, 2], F32)
                jb["stage"] = (mk(nc.sbuf_tensor, f"{tag}_stage",
                                  [128, 2 * kd], BF16)
                               if jb["sink"] == "dma" else None)
                jb["ps_tr"] = mk(nc.psum_tensor, f"{tag}_ptr", [128, kd], BF16)
                jb["s_ld"] = [mks(f"{tag}_ld0"), mks(f"{tag}_ld1"),
                              mks(f"{tag}_ld2")]
                jb["s_ab"] = mks(f"{tag}_ab")
                jb["s_ac"] = mks(f"{tag}_ac")
                jb["s_qd"] = mks(f"{tag}_qd")
                jb["s_tr"] = mks(f"{tag}_tr")
                jb["s_ev"] = mks(f"{tag}_ev")
                jb["s_dm"] = [mks(f"{tag}_dm0"), mks(f"{tag}_dm1")]
            ps_m = mk(nc.psum_tensor, "wq_pm", [1, 1], F32)
            ps_b = mk(nc.psum_tensor, "wq_pb", [128, 1], F32)
            s_mn = mks("wq_mn")
            s_tm = mks("wq_tm")

            sched = []
            for k in range(8):
                sched.append((jobs[0], k))
                for q in range(4):
                    sched.append((jobs[1], 4 * k + q))

            def all_loads(eng, jb):
                nch, kd = jb["nch"], jb["kd"]
                for i in range(nch):
                    if i >= 3:
                        eng.wait_ge(jb["s_ab"], i - 2)
                    eng.dma_start(
                        jb["wf"][:, (i % 3) * kd:(i % 3 + 1) * kd],
                        jb["src"][i * 128:(i + 1) * 128, :]
                    ).then_inc(jb["s_ld"][i % 3], 16)
                for j in range(nch):
                    k = nch + j
                    if k - 3 < nch:
                        eng.wait_ge(jb["s_ab"], k - 2)
                    if j >= 3:
                        eng.wait_ge(jb["s_qd"], j - 2)
                    eng.dma_start(
                        jb["wf"][:, (k % 3) * kd:(k % 3 + 1) * kd],
                        jb["src"][j * 128:(j + 1) * 128, :]
                    ).then_inc(jb["s_ld"][k % 3], 16)

            with nc.Block() as blk:
                @blk.sync
                def _(sy):
                    all_loads(sy, jobs[0])

                @blk.gpsimd
                def _(g):
                    all_loads(g, jobs[1])

                @blk.scalar
                def _(sc):
                    for jb, i in sched:
                        kd = jb["kd"]
                        sc.wait_ge(jb["s_ld"][i % 3], 16 * (i // 3 + 1))
                        if i >= 4:
                            sc.wait_ge(jb["s_ac"], i - 3)
                        sc.activation(jb["a1m"][:, :],
                                      jb["wf"][:, (i % 3) * kd:(i % 3 + 1) * kd],
                                      AF.Abs,
                                      accum_out=jb["csum"][:, i % 4:i % 4 + 1]
                                      ).then_inc(jb["s_ab"], 1)
                    jb = jobs[1]
                    kd, nb, nch = jb["kd"], jb["nb"], jb["nch"]
                    for i in range(nch):
                        sc.wait_ge(jb["s_ev"], i + 1)
                        sc.dma_start(
                            bass.AP(t1T_s, i * 128,
                                    [[H, 128], [128 * H, nb], [1, 128]]),
                            bass.AP(jb["stage"], (i % 2) * kd,
                                    [[2 * kd, 128], [128, nb], [1, 128]]),
                        ).then_inc(jb["s_dm"][i % 2], 16)
                    sc.wait_ge(jb["s_dm"][0], 16 * ((nch + 1) // 2))
                    sc.wait_ge(jb["s_dm"][1], 16 * (nch // 2))

                @blk.tensor
                def _(t):
                    t.wait_ge(jobs[0]["s_ac"], jobs[0]["nch"])
                    t.matmul(ps_m[0:1, 0:1], jobs[0]["acc"][:, :],
                             ones_col_f[:, :], start=True,
                             stop=True).then_inc(s_tm, 1)
                    t.wait_ge(jobs[1]["s_ac"], jobs[1]["nch"])
                    t.wait_ge(s_mn, 1)
                    t.matmul(ps_m[0:1, 0:1], jobs[1]["acc"][:, :],
                             ones_col_f[:, :], start=True,
                             stop=True).then_inc(s_tm, 1)
                    t.matmul(ps_b[:, 0:1], ones_row_f[0:1, :],
                             jobs[0]["msb"][0:1, 0:1], start=True,
                             stop=True).then_inc(s_tm, 1)
                    t.wait_ge(s_mn, 3)
                    t.matmul(ps_b[:, 0:1], ones_row_f[0:1, :],
                             jobs[1]["msb"][0:1, 0:1], start=True,
                             stop=True).then_inc(s_tm, 1)
                    for jb, i in sched:
                        kd, nb = jb["kd"], jb["nb"]
                        t.wait_ge(jb["s_qd"], i + 1)
                        if i >= 1:
                            t.wait_ge(jb["s_ev"], i)
                        ins = None
                        for b in range(nb):
                            ins = t.transpose(
                                jb["ps_tr"][:, b * 128:(b + 1) * 128],
                                jb["tern"][:, (i % 2) * kd + b * 128:
                                           (i % 2) * kd + (b + 1) * 128],
                                ident_b[:, :])
                        ins.then_inc(jb["s_tr"], 1)

                @blk.vector
                def _(v):
                    for jb, i in sched:
                        v.wait_ge(jb["s_ab"], i + 1)
                        if i == 0:
                            ins = v.tensor_copy(jb["acc"][:, :],
                                                jb["csum"][:, 0:1])
                        else:
                            ins = v.tensor_tensor(jb["acc"][:, :],
                                                  jb["acc"][:, :],
                                                  jb["csum"][:, i % 4:i % 4 + 1],
                                                  OP.add)
                        ins.then_inc(jb["s_ac"], 1)
                        v.wait_ge(jb["s_ac"], i + 1)
                    for jx, jb in enumerate(jobs):
                        v.wait_ge(s_tm, jx + 1)
                        v.tensor_copy(jb["msb"][0:1, 0:1], ps_m[0:1, 0:1]
                                      ).then_inc(s_mn, 1)
                    for jx, jb in enumerate(jobs):
                        v.wait_ge(s_tm, jx + 3)
                        v.tensor_scalar(jb["mcol"][:, :], ps_b[:, 0:1],
                                        1.0 / (jb["nch"] * 128.0 * jb["kd"]),
                                        QCLIP, OP.mult, OP.max
                                        ).then_inc(s_mn, 1)
                        v.wait_ge(s_mn, 2 * jx + 3)
                        v.tensor_scalar(jb["thr"][:, 0:1], jb["mcol"][:, :],
                                        0.5, None, OP.mult, OP.bypass)
                        v.tensor_scalar(jb["thr"][:, 1:2], jb["mcol"][:, :],
                                        -0.5, None, OP.mult, OP.bypass
                                        ).then_inc(s_mn, 1)
                        v.wait_ge(s_mn, 2 * jx + 4)
                    def evac(jb, k):
                        kd, nb = jb["kd"], jb["nb"]
                        v.wait_ge(jb["s_tr"], k + 1)
                        if jb["sink"] == "resident":
                            v.tensor_copy(
                                bass.AP(t2w_sb, k * 128,
                                        [[32 * 1024, 128], [1024, 32],
                                         [1, 128]]),
                                bass.AP(jb["ps_tr"], 0,
                                        [[kd, 128], [128, nb], [1, 128]]),
                            ).then_inc(jb["s_ev"], 1)
                        else:
                            if k >= 2:
                                v.wait_ge(jb["s_dm"][k % 2], 16 * (k // 2))
                            v.tensor_copy(
                                jb["stage"][:, (k % 2) * kd:(k % 2 + 1) * kd],
                                jb["ps_tr"][:, :]).then_inc(jb["s_ev"], 1)

                    for jb, j in sched:
                        kd, nch = jb["kd"], jb["nch"]
                        k = nch + j
                        ca = len([x for x in range(nch) if x % 3 == k % 3])
                        v.wait_ge(jb["s_ld"][k % 3], 16 * (ca + j // 3 + 1))
                        if j >= 2:
                            v.wait_ge(jb["s_tr"], j - 1)
                        src = jb["wf"][:, (k % 3) * kd:(k % 3 + 1) * kd]
                        v.tensor_scalar(jb["a1m"][:, :], src,
                                        jb["thr"][:, 0:1], -1.0,
                                        OP.is_gt, OP.add)
                        v.scalar_tensor_tensor(
                            jb["tern"][:, (j % 2) * kd:(j % 2 + 1) * kd], src,
                            jb["thr"][:, 1:2], jb["a1m"][:, :],
                            OP.is_ge, OP.add).then_inc(jb["s_qd"], 1)
                        if j >= 1:
                            evac(jb, j - 1)
                    for jb in jobs:
                        evac(jb, jb["nch"] - 1)

            nc.clear_and_free_semaphores(semh)
            nc.all_engine_barrier()
            for h in reversed(stack):
                h.__exit__(None, None, None)

        emit_wquant_jobs([
            {"tag": "w2q", "src": w2_h, "nch": 8, "kd": H,
             "mcol": m2_col, "sink": "resident"},
            {"tag": "w1q", "src": w1_h, "nch": 32, "kd": C,
             "mcol": m1_col, "sink": "dma"},
        ])

        with contextlib.ExitStack() as es2:
            sem2 = sem_scope(nc, es2)
            wq_sb = es2.enter_context(nc.sbuf_tensor("wq_sb", [128, 8 * M3], BF16))
            bqkv_r = es2.enter_context(nc.sbuf_tensor("bqkv_r", [1, M3], BF16))
            x_bf = es2.enter_context(nc.sbuf_tensor("x_bf", [128, 2 * C], BF16))
            xT_sb = es2.enter_context(nc.sbuf_tensor("xT_sb", [128, 2 * C], BF16))
            q_f = es2.enter_context(nc.sbuf_tensor("q_f", [128, 2 * C], F32))
            k_f = es2.enter_context(nc.sbuf_tensor("k_f", [128, 2 * C], F32))
            q2_f = es2.enter_context(nc.sbuf_tensor("q2_f", [128, 2 * C], F32))
            k2_f = es2.enter_context(nc.sbuf_tensor("k2_f", [128, 2 * C], F32))
            v_bf = es2.enter_context(nc.sbuf_tensor("v_bf", [128, 2 * C], BF16))
            q2f_bf = es2.enter_context(nc.sbuf_tensor("q2f_bf", [128, 2 * C], BF16))
            k2n_bf = es2.enter_context(nc.sbuf_tensor("k2n_bf", [128, 2 * C], BF16))
            k2sq_bf = es2.enter_context(nc.sbuf_tensor("k2sq_bf", [128, 2 * C], BF16))
            stat = es2.enter_context(nc.sbuf_tensor("stat", [128, 32], F32))
            ps_xT = es2.enter_context(nc.psum_tensor("ps_xT", [128, 2 * C], BF16))
            ps_q0 = es2.enter_context(nc.psum_tensor("ps_q0", [128, 512], F32))
            ps_q1 = es2.enter_context(nc.psum_tensor("ps_q1", [128, 512], F32))
            ps_q2 = es2.enter_context(nc.psum_tensor("ps_q2", [128, 512], F32))
            ps_rc0 = es2.enter_context(nc.psum_tensor("ps_rc0", [1, 512], F32))
            ps_rc1 = es2.enter_context(nc.psum_tensor("ps_rc1", [1, 512], F32))
            ps_rcT = es2.enter_context(nc.psum_tensor("ps_rcT", [128, 8], F32))
            rrow = es2.enter_context(nc.sbuf_tensor("rrow", [1, C], F32))
            s_ldw = sem2("a_ldw")
            s_ldx = [sem2("a_ldx0"),
                     sem2("a_ldx1")]
            s_xtr = sem2("a_xtr")
            s_xev = sem2("a_xev")
            s_mmq = sem2("a_mmq")
            s_qev = sem2("a_qev")
            s_qst = sem2("a_qst")
            s_sqs = sem2("a_sqs")
            s_fin = sem2("a_fin")
            s_spq = [sem2("a_spq0"),
                     sem2("a_spq1")]
            s_spk = [sem2("a_spk0"),
                     sem2("a_spk1")]
            s_spv = [sem2("a_spv0"),
                     sem2("a_spv1")]
            s_rcm = sem2("a_rcm")
            s_vss = sem2("a_vss")
            s_vs2 = sem2("a_vs2")
            ps_qkv = [ps_q0, ps_q1, ps_q2]
            with nc.Block() as blk:
                @blk.sync
                def _(sy):
                    for ip in range(NT):
                        pp = ip % 2
                        sy.wait_ge(s_fin, 3 * ip + 1)
                        sy.dma_start(q2f_s[ip * 128:(ip + 1) * 128, :],
                                     q2f_bf[:, pp * C:(pp + 1) * C]
                                     ).then_inc(s_spq[pp], 16)
                        sy.wait_ge(s_fin, 3 * ip + 3)
                        sy.dma_start(k2n_s[ip * 128:(ip + 1) * 128, :],
                                     k2n_bf[:, pp * C:(pp + 1) * C]
                                     ).then_inc(s_spk[pp], 16)
                        sy.wait_ge(s_qev, 6 * ip + 6)
                        sy.dma_start(v_s[ip * 128:(ip + 1) * 128, :],
                                     v_bf[:, pp * C:(pp + 1) * C]
                                     ).then_inc(s_spv[pp], 16)
                    for sm in (s_spq, s_spk, s_spv):
                        sy.wait_ge(sm[0], 16 * (NT // 2))
                        sy.wait_ge(sm[1], 16 * (NT // 2))

                @blk.gpsimd
                def _(g):
                    for j in range(8):
                        g.dma_start(wq_sb[:, j * M3:(j + 1) * M3],
                                    wqkv_h[j * 128:(j + 1) * 128, :]
                                    ).then_inc(s_ldw, 16)
                    g.dma_start(bqkv_r[:, :], bqkv_h[:, :]).then_inc(s_ldw, 16)
                    for i in range(NT):
                        if i >= 2:
                            g.wait_ge(s_xtr, i - 1)
                        g.dma_start(x_bf[:, (i % 2) * C:(i % 2 + 1) * C],
                                    x_h[i * 128:(i + 1) * 128, :]
                                    ).then_inc(s_ldx[i % 2], 16)

                @blk.tensor
                def _(t):
                    def rc_mm(k):
                        pk = k % 2
                        t.wait_ge(s_fin, 3 * k + 3)
                        t.matmul(ps_rc0[0:1, 0:512], ones_col_b[:, :],
                                 k2sq_bf[:, pk * C:pk * C + 512],
                                 start=(k == 0), stop=(k == NT - 1))
                        t.matmul(ps_rc1[0:1, 0:512], ones_col_b[:, :],
                                 k2sq_bf[:, pk * C + 512:pk * C + 1024],
                                 start=(k == 0), stop=(k == NT - 1)
                                 ).then_inc(s_rcm, 1)

                    t.wait_ge(s_ldw, 16 * 9)
                    for i in range(NT):
                        par = i % 2
                        t.wait_ge(s_ldx[par], 16 * (i // 2 + 1))
                        if i >= 2:
                            t.wait_ge(s_xev, i - 1)
                        ins = None
                        for b in range(8):
                            ins = t.transpose(ps_xT[:, par * C + b * 128:
                                                    par * C + (b + 1) * 128],
                                              x_bf[:, par * C + b * 128:
                                                   par * C + (b + 1) * 128],
                                              ident_b[:, :])
                        ins.then_inc(s_xtr, 1)
                        t.wait_ge(s_xev, i + 1)
                        for p in range(6):
                            gidx = 6 * i + p
                            if gidx >= 3:
                                t.wait_ge(s_qev, gidx - 2)
                            pb = ps_qkv[p % 3]
                            for cch in range(8):
                                t.matmul(pb[:, 0:512],
                                         xT_sb[:, par * C + cch * 128:
                                               par * C + (cch + 1) * 128],
                                         wq_sb[:, cch * M3 + p * 512:
                                               cch * M3 + (p + 1) * 512],
                                         start=(cch == 0), stop=False)
                            t.matmul(pb[:, 0:512], ones_row_b[0:1, :],
                                     bqkv_r[0:1, p * 512:(p + 1) * 512],
                                     start=False, stop=True).then_inc(s_mmq, 1)
                        if i >= 1:
                            rc_mm(i - 1)
                    rc_mm(NT - 1)

                @blk.vector
                def _(v):
                    vss = SelfSync(v, s_vss)
                    for i in range(NT):
                        par = i % 2
                        sb = 16 * par
                        for p in range(6):
                            v.wait_ge(s_mmq, 6 * i + p + 1)
                            pb = ps_qkv[p % 3]
                            if p < 2:
                                if i >= 2 and p == 0:
                                    v.wait_ge(s_sqs, 4 * (i - 2) + 2)
                                v.tensor_copy(q_f[:, par * C + p * 512:
                                                  par * C + (p + 1) * 512],
                                              pb[:, 0:512]).then_inc(s_qev, 1)
                            elif p < 4:
                                if i >= 2 and p == 2:
                                    v.wait_ge(s_sqs, 4 * (i - 2) + 4)
                                v.tensor_copy(k_f[:, par * C + (p - 2) * 512:
                                                  par * C + (p - 1) * 512],
                                              pb[:, 0:512]).then_inc(s_qev, 1)
                            else:
                                if i >= 2 and p == 4:
                                    v.wait_ge(s_spv[par], 16 * (i // 2))
                                v.tensor_copy(v_bf[:, par * C + (p - 4) * 512:
                                                   par * C + (p - 3) * 512],
                                              pb[:, 0:512]).then_inc(s_qev, 1)
                        vss.sync(v.reduce_sum(stat[:, sb + 0:sb + 1],
                                              q_f[:, par * C:(par + 1) * C],
                                              AX.X))
                        v.tensor_scalar(stat[:, sb + 1:sb + 2],
                                        stat[:, sb + 0:sb + 1], -INV1024, None,
                                        OP.mult, OP.bypass).then_inc(s_qst, 1)
                        vss.sync(v.reduce_sum(stat[:, sb + 11:sb + 12],
                                              k_f[:, par * C:(par + 1) * C],
                                              AX.X))
                        v.tensor_scalar(stat[:, sb + 12:sb + 13],
                                        stat[:, sb + 11:sb + 12], -INV1024, None,
                                        OP.mult, OP.bypass).then_inc(s_qst, 1)
                        v.wait_ge(s_sqs, 4 * i + 3)
                        vss.sync(v.tensor_scalar(stat[:, sb + 5:sb + 6],
                                                 stat[:, sb + 2:sb + 3],
                                                 EPS_SUM, None,
                                                 OP.add, OP.bypass))
                        vss.sync(v.reciprocal(stat[:, sb + 6:sb + 7],
                                              stat[:, sb + 5:sb + 6]))
                        vss.sync(v.tensor_tensor(stat[:, sb + 7:sb + 8],
                                                 stat[:, sb + 4:sb + 5],
                                                 stat[:, sb + 6:sb + 7],
                                                 OP.mult))
                        vss.sync(v.tensor_scalar(stat[:, sb + 8:sb + 9],
                                                 stat[:, sb + 7:sb + 8],
                                                 EPS_NORM, None,
                                                 OP.max, OP.bypass))
                        vss.sync(v.tensor_tensor(stat[:, sb + 9:sb + 10],
                                                 stat[:, sb + 5:sb + 6],
                                                 stat[:, sb + 8:sb + 9],
                                                 OP.mult))
                        vss.sync(v.reciprocal(stat[:, sb + 10:sb + 11],
                                              stat[:, sb + 9:sb + 10]))
                        if i >= 2:
                            v.wait_ge(s_spq[par], 16 * (i // 2))
                        v.tensor_scalar(q2f_bf[:, par * C:(par + 1) * C],
                                        q2_f[:, par * C:(par + 1) * C],
                                        stat[:, sb + 10:sb + 11],
                                        INV64, OP.mult, OP.mult).then_inc(s_fin, 1)
                        v.wait_ge(s_sqs, 4 * i + 4)
                        vss.sync(v.tensor_scalar(stat[:, sb + 14:sb + 15],
                                                 stat[:, sb + 13:sb + 14],
                                                 EPS_SUM, None,
                                                 OP.add, OP.bypass))
                        vss.sync(v.reciprocal(stat[:, sb + 15:sb + 16],
                                              stat[:, sb + 14:sb + 15]))
                        if i >= 2:
                            v.wait_ge(s_spk[par], 16 * (i // 2))
                        v.tensor_scalar(k2n_bf[:, par * C:(par + 1) * C],
                                        k2_f[:, par * C:(par + 1) * C],
                                        stat[:, sb + 15:sb + 16],
                                        None, OP.mult, OP.bypass).then_inc(s_fin, 1)
                        if i >= 2:
                            v.wait_ge(s_rcm, i - 1)
                        v.tensor_tensor(k2sq_bf[:, par * C:(par + 1) * C],
                                        k2n_bf[:, par * C:(par + 1) * C],
                                        k2n_bf[:, par * C:(par + 1) * C],
                                        OP.mult).then_inc(s_fin, 1)

                @blk.scalar
                def _(sc):
                    def stats(k):
                        pk = k % 2
                        sb = 16 * pk
                        sc.wait_ge(s_qst, 2 * k + 1)
                        if k >= 2:
                            sc.wait_ge(s_fin, 3 * (k - 1) + 1)
                        qs_ = q_f[:, pk * C:(pk + 1) * C]
                        q2s_ = q2_f[:, pk * C:(pk + 1) * C]
                        sc.activation(q2s_, qs_, AF.Square,
                                      bias=stat[:, sb + 1:sb + 2],
                                      accum_out=stat[:, sb + 2:sb + 3]
                                      ).then_inc(s_sqs, 1)
                        sc.activation(qs_, q2s_, AF.Square,
                                      accum_out=stat[:, sb + 3:sb + 4]
                                      ).then_inc(s_sqs, 1)
                        sc.activation(stat[:, sb + 4:sb + 5],
                                      stat[:, sb + 3:sb + 4], AF.Sqrt
                                      ).then_inc(s_sqs, 1)
                        sc.wait_ge(s_qst, 2 * k + 2)
                        if k >= 2:
                            sc.wait_ge(s_fin, 3 * (k - 1) + 2)
                        sc.activation(k2_f[:, pk * C:(pk + 1) * C],
                                      k_f[:, pk * C:(pk + 1) * C], AF.Square,
                                      bias=stat[:, sb + 12:sb + 13],
                                      accum_out=stat[:, sb + 13:sb + 14]
                                      ).then_inc(s_sqs, 1)

                    for i in range(NT):
                        par = i % 2
                        sc.wait_ge(s_xtr, i + 1)
                        if i >= 2:
                            sc.wait_ge(s_mmq, 6 * (i - 1))
                        sc.copy(xT_sb[:, par * C:(par + 1) * C],
                                ps_xT[:, par * C:(par + 1) * C]
                                ).then_inc(s_xev, 1)
                        if i >= 1:
                            stats(i - 1)
                    stats(NT - 1)

            with nc.Block() as blk:
                @blk.vector
                def _(v):
                    v.tensor_copy(rrow[0:1, 0:512], ps_rc0[0:1, 0:512])
                    v.tensor_copy(rrow[0:1, 512:1024], ps_rc1[0:1, 0:512])
            with nc.Block() as blk:
                @blk.scalar
                def _(sc):
                    sc.activation(rrow[0:1, :], rrow[0:1, :], AF.Sqrt)
            with nc.Block() as blk:
                @blk.vector
                def _(v):
                    v.tensor_scalar(rrow[0:1, :], rrow[0:1, :], EPS_NORM, None,
                                    OP.max, OP.bypass).then_inc(s_vs2, 1)
                    v.wait_ge(s_vs2, 1)
                    v.reciprocal(rrow[0:1, :], rrow[0:1, :])
            with nc.Block() as blk:
                @blk.tensor
                def _(t):
                    for cch in range(8):
                        t.transpose(ps_rcT[:, cch:cch + 1],
                                    rrow[0:1, cch * 128:(cch + 1) * 128],
                                    ident_f[0:1, 0:1])
            with nc.Block() as blk:
                @blk.vector
                def _(v):
                    v.tensor_copy(rcol[:, :], ps_rcT[:, :])

        with contextlib.ExitStack() as es3:
            sem3 = sem_scope(nc, es3)
            wln_sb = es3.enter_context(nc.sbuf_tensor("wln_sb", [128, 8 * C], BF16))
            bln_r = es3.enter_context(nc.sbuf_tensor("bln_r", [1, C], BF16))
            kv_sb = es3.enter_context(nc.sbuf_tensor("kv_sb", [128, 8 * C], BF16))
            s_lk = [sem3("b_lk0"), sem3("b_lk1"),
                    sem3("b_lk2"), sem3("b_lk3")]
            s_lv = [sem3("b_lv0"), sem3("b_lv1"),
                    sem3("b_lv2"), sem3("b_lv3")]
            s_mkv = sem3("b_mkv")
            s_kev = sem3("b_kev")
            s_lwl = sem3("b_lwl")
            es3b = es3.enter_context(contextlib.ExitStack())
            k2n_t = es3b.enter_context(nc.sbuf_tensor("k2n_t", [128, 4 * C], BF16))
            vh_t = es3b.enter_context(nc.sbuf_tensor("vh_t", [128, 4 * 512], BF16))
            ps_kv = es3b.enter_context(nc.psum_tensor("ps_kv", [128, 8 * 512], F32))
            with nc.Block() as blk:
                @blk.gpsimd
                def _(g):
                    for j in range(8):
                        g.dma_start(wln_sb[:, j * C:(j + 1) * C],
                                    wln_h[j * 128:(j + 1) * 128, :]
                                    ).then_inc(s_lwl, 16)
                    g.dma_start(bln_r[:, :], bln_h[:, :]).then_inc(s_lwl, 16)
                    for hh in range(2):
                        for i in range(NT):
                            gi = hh * NT + i
                            if gi >= 4:
                                g.wait_ge(s_mkv, gi - 3)
                            g.dma_start(
                                vh_t[:, (gi % 4) * 512:(gi % 4 + 1) * 512],
                                bass.AP(v_s, i * 128 * C + hh * 512,
                                        [[C, 128], [1, 512]]),
                            ).then_inc(s_lv[gi % 4], 16)
                    g.wait_ge(s_lwl, 16 * 9)

                @blk.sync
                def _(sy):
                    for hh in range(2):
                        for i in range(NT):
                            gi = hh * NT + i
                            if gi >= 4:
                                sy.wait_ge(s_mkv, gi - 3)
                            sy.dma_start(k2n_t[:, (gi % 4) * C:(gi % 4 + 1) * C],
                                         k2n_s[i * 128:(i + 1) * 128, :]
                                         ).then_inc(s_lk[gi % 4], 16)

                @blk.tensor
                def _(t):
                    for hh in range(2):
                        if hh == 1:
                            t.wait_ge(s_kev, 1)
                        for i in range(NT):
                            gi = hh * NT + i
                            t.wait_ge(s_lk[gi % 4], 16 * (gi // 4 + 1))
                            t.wait_ge(s_lv[gi % 4], 16 * (gi // 4 + 1))
                            ins = None
                            for cch in range(8):
                                ins = t.matmul(
                                    ps_kv[:, cch * 512:(cch + 1) * 512],
                                    k2n_t[:, (gi % 4) * C + cch * 128:
                                          (gi % 4) * C + (cch + 1) * 128],
                                    vh_t[:, (gi % 4) * 512:(gi % 4 + 1) * 512],
                                    start=(i == 0), stop=(i == NT - 1))
                            ins.then_inc(s_mkv, 1)

                @blk.vector
                def _(v):
                    for hh in range(2):
                        v.wait_ge(s_mkv, (hh + 1) * NT)
                        ins = None
                        for cch in range(8):
                            ins = v.tensor_scalar(
                                kv_sb[:, cch * C + hh * 512:
                                      cch * C + (hh + 1) * 512],
                                ps_kv[:, cch * 512:(cch + 1) * 512],
                                rcol[:, cch:cch + 1], None, OP.mult, OP.bypass)
                        ins.then_inc(s_kev, 1)

            es3b.close()
            with contextlib.ExitStack() as es4:
                sem4 = sem_scope(nc, es4)
                x_f2 = es4.enter_context(nc.sbuf_tensor("x_f2", [128, 2 * C], F32))
                q2f_t = es4.enter_context(nc.sbuf_tensor("q2f_t", [128, 2 * C], BF16))
                v_t2 = es4.enter_context(nc.sbuf_tensor("v_t2", [128, 2 * C], BF16))
                q2fT = es4.enter_context(nc.sbuf_tensor("q2fT", [128, 2 * C], BF16))
                vt2 = es4.enter_context(nc.sbuf_tensor("vt2", [128, 2 * C], BF16))
                vt2T = es4.enter_context(nc.sbuf_tensor("vt2T", [128, 2 * C], BF16))
                r1_f = es4.enter_context(nc.sbuf_tensor("r1_f", [128, 3 * C], F32))
                x2_f = es4.enter_context(nc.sbuf_tensor("x2_f", [128, 2 * C], F32))
                sqscr = es4.enter_context(nc.sbuf_tensor("sqscr", [128, C], F32))
                xq_i = es4.enter_context(nc.sbuf_tensor("xq_i", [128, C], I32))
                u2_f = es4.enter_context(nc.sbuf_tensor("u2_f", [128, C], F32))
                corr_i = es4.enter_context(nc.sbuf_tensor("corr_i", [128, C], I32))
                xq_bf = es4.enter_context(nc.sbuf_tensor("xq_bf", [128, 2 * C], BF16))
                xqT_ev = es4.enter_context(nc.sbuf_tensor("xqT_ev", [128, 2 * C], BF16))
                st2 = es4.enter_context(nc.sbuf_tensor("st2", [128, 24], F32))
                st3 = es4.enter_context(nc.sbuf_tensor("st3", [128, 64], F32))
                ps_tr2 = es4.enter_context(nc.psum_tensor("ps_tr2", [128, 3 * C], BF16))
                ps_t2a = es4.enter_context(nc.psum_tensor("ps_t2a", [128, 512], F32))
                ps_t2b = es4.enter_context(nc.psum_tensor("ps_t2b", [128, 512], F32))
                ps_ata = es4.enter_context(nc.psum_tensor("ps_ata", [128, 512], F32))
                ps_atb = es4.enter_context(nc.psum_tensor("ps_atb", [128, 512], F32))
                ps_dq = es4.enter_context(nc.psum_tensor("ps_dq", [1, 128], F32))
                s_lxf = [sem4("c_lxf0"),
                         sem4("c_lxf1")]
                s_lqf = [sem4("c_lqf0"),
                         sem4("c_lqf1")]
                s_lvt = [sem4("c_lvt0"),
                         sem4("c_lvt1")]
                s_trq = sem4("c_trq")
                s_teq = sem4("c_teq")
                s_trv = sem4("c_trv")
                s_tev = sem4("c_tev")
                s_trx = sem4("c_trx")
                s_tex = sem4("c_tex")
                s_mt2 = sem4("c_mt")
                s_ma = sem4("c_ma")
                s_ev2 = sem4("c_ev")
                s_sq = sem4("c_sq")
                s_var = sem4("c_var")
                s_rs = sem4("c_rs")
                s_qx = sem4("c_qx")
                s_x2v = sem4("c_x2v")
                s_dqt = sem4("c_dqt")
                s_dqe = sem4("c_dqe")
                s_cvs = sem4("c_cvs")
                s_spx2 = [sem4("c_spx0"),
                          sem4("c_spx1")]
                s_spxq = [sem4("c_spq0"),
                          sem4("c_spq1")]
                ps_t2 = [ps_t2a, ps_t2b]
                ps_at = [ps_ata, ps_atb]
                NS4 = NT + 4
                with nc.Block() as blk:
                    @blk.sync
                    def _(sy):
                        for i in range(NT):
                            par = i % 2
                            if i >= 2:
                                sy.wait_ge(s_trq, i - 1)
                                sy.wait_ge(s_ev2, 4 * i - 4)
                            sy.dma_start(x_f2[:, par * C:(par + 1) * C],
                                         x_h[i * 128:(i + 1) * 128, :]
                                         ).then_inc(s_lxf[par], 16)
                            sy.dma_start(q2f_t[:, par * C:(par + 1) * C],
                                         q2f_s[i * 128:(i + 1) * 128, :]
                                         ).then_inc(s_lqf[par], 16)
                            sy.dma_start(v_t2[:, par * C:(par + 1) * C],
                                         v_s[i * 128:(i + 1) * 128, :]
                                         ).then_inc(s_lvt[par], 16)

                    @blk.tensor
                    def _(t):
                        t.wait_ge(s_kev, 2)
                        t.wait_ge(s_lwl, 16 * 9)
                        for i in range(NS4):
                            par = i % 2
                            if i < NT:
                                t.wait_ge(s_lxf[par], 16 * (i // 2 + 1))
                                t.wait_ge(s_lqf[par], 16 * (i // 2 + 1))
                                t.wait_ge(s_lvt[par], 16 * (i // 2 + 1))
                                if i >= 1:
                                    t.wait_ge(s_teq, i)
                                ins = None
                                for b in range(8):
                                    ins = t.transpose(
                                        ps_tr2[:, b * 128:(b + 1) * 128],
                                        q2f_t[:, par * C + b * 128:
                                              par * C + (b + 1) * 128],
                                        ident_b[:, :])
                                ins.then_inc(s_trq, 1)
                                t.wait_ge(s_teq, i + 1)
                                for dh in range(2):
                                    if i >= 1:
                                        t.wait_ge(s_ev2, 4 * (i - 1) + dh + 1)
                                    ins = None
                                    for cch in range(8):
                                        ins = t.matmul(
                                            ps_t2[dh][:, 0:512],
                                            q2fT[:, par * C + cch * 128:
                                                 par * C + (cch + 1) * 128],
                                            kv_sb[:, cch * C + dh * 512:
                                                  cch * C + (dh + 1) * 512],
                                            start=(cch == 0), stop=(cch == 7))
                                    ins.then_inc(s_mt2, 1)
                                t.wait_ge(s_ev2, 4 * i + 2)
                                if i >= 1:
                                    t.wait_ge(s_tev, i)
                                ins = None
                                for b in range(8):
                                    ins = t.transpose(
                                        ps_tr2[:, C + b * 128:C + (b + 1) * 128],
                                        vt2[:, par * C + b * 128:
                                            par * C + (b + 1) * 128],
                                        ident_b[:, :])
                                ins.then_inc(s_trv, 1)
                                t.wait_ge(s_tev, i + 1)
                                for eh in range(2):
                                    if i >= 1:
                                        t.wait_ge(s_ev2, 4 * (i - 1) + eh + 3)
                                    for dch in range(8):
                                        t.matmul(ps_at[eh][:, 0:512],
                                                 vt2T[:, par * C + dch * 128:
                                                      par * C + (dch + 1) * 128],
                                                 wln_sb[:, dch * C + eh * 512:
                                                        dch * C + (eh + 1) * 512],
                                                 start=(dch == 0), stop=False)
                                    t.matmul(ps_at[eh][:, 0:512],
                                             ones_row_b[0:1, :],
                                             bln_r[0:1, eh * 512:(eh + 1) * 512],
                                             start=False,
                                             stop=True).then_inc(s_ma, 1)
                            k = i - 4
                            if 0 <= k < NT:
                                pk = k % 2
                                b3 = 8 * (k % 8)
                                t.wait_ge(s_qx, 2 * k + 1)
                                if k >= 1:
                                    t.wait_ge(s_dqe, k)
                                t.transpose(ps_dq[0:1, :], st3[:, b3 + 5:b3 + 6],
                                            ident_f[:, :]).then_inc(s_dqt, 1)
                                t.wait_ge(s_qx, 2 * k + 2)
                                if k >= 1:
                                    t.wait_ge(s_tex, k)
                                ins = None
                                for b in range(8):
                                    ins = t.transpose(
                                        ps_tr2[:, 2 * C + b * 128:
                                               2 * C + (b + 1) * 128],
                                        xq_bf[:, pk * C + b * 128:
                                              pk * C + (b + 1) * 128],
                                        ident_b[:, :])
                                ins.then_inc(s_trx, 1)

                    @blk.vector
                    def _(v):
                        vss = SelfSync(v, s_cvs)

                        def v_quant_a(k):
                            p2 = k % 2
                            p3 = k % 3
                            s2 = 6 * (k % 4)
                            b3 = 8 * (k % 8)
                            v.wait_ge(s_rs, k + 1)
                            vss.sync(v.reciprocal(st3[:, b3 + 6:b3 + 7],
                                                  st2[:, s2 + 5:s2 + 6]))
                            if k >= 2:
                                v.wait_ge(s_spx2[p2], 16 * (k // 2))
                            v.tensor_scalar(x2_f[:, p2 * C:(p2 + 1) * C],
                                            r1_f[:, p3 * C:(p3 + 1) * C],
                                            st2[:, s2 + 2:s2 + 3],
                                            st3[:, b3 + 6:b3 + 7],
                                            OP.add, OP.mult).then_inc(s_x2v, 1)
                            vss.sync(v.reduce_max(st3[:, b3 + 0:b3 + 1],
                                                  x2_f[:, p2 * C:(p2 + 1) * C],
                                                  AX.X,
                                                  apply_absolute_value=True))

                        def v_quant_b(k):
                            p2 = k % 2
                            b3 = 8 * (k % 8)
                            vss.sync(v.tensor_scalar(st3[:, b3 + 1:b3 + 2],
                                                     st3[:, b3 + 0:b3 + 1],
                                                     QCLIP, None,
                                                     OP.max, OP.bypass))
                            vss.sync(v.reciprocal(st3[:, b3 + 2:b3 + 3],
                                                  st3[:, b3 + 1:b3 + 2]))
                            vss.sync(v.tensor_scalar(st3[:, b3 + 3:b3 + 4],
                                                     st3[:, b3 + 2:b3 + 3],
                                                     127.0, None,
                                                     OP.mult, OP.bypass))
                            vss.sync(v.tensor_scalar(st3[:, b3 + 4:b3 + 5],
                                                     st3[:, b3 + 2:b3 + 3],
                                                     254.0, None,
                                                     OP.mult, OP.bypass))
                            vss.sync(v.tensor_scalar(st3[:, b3 + 5:b3 + 6],
                                                     st3[:, b3 + 1:b3 + 2],
                                                     m1_col[:, :], INV127,
                                                     OP.mult, OP.mult))
                            v.nop().then_inc(s_qx, 1)
                            v.tensor_scalar(xq_i[:, :],
                                            x2_f[:, p2 * C:(p2 + 1) * C],
                                            st3[:, b3 + 3:b3 + 4], None,
                                            OP.mult, OP.bypass)
                            v.tensor_scalar(u2_f[:, :],
                                            x2_f[:, p2 * C:(p2 + 1) * C],
                                            st3[:, b3 + 4:b3 + 5], None,
                                            OP.mult, OP.bypass)
                            v.scalar_tensor_tensor(u2_f[:, :], xq_i[:, :],
                                                   -2.0, u2_f[:, :],
                                                   OP.mult, OP.add)
                            v.scalar_tensor_tensor(corr_i[:, :], u2_f[:, :],
                                                   -1.0, xq_i[:, :],
                                                   OP.is_lt, OP.subtract)
                            if k >= 2:
                                v.wait_ge(s_trx, k - 1)
                            v.scalar_tensor_tensor(
                                xq_bf[:, p2 * C:(p2 + 1) * C],
                                u2_f[:, :], 1.0, corr_i[:, :],
                                OP.is_gt, OP.subtract).then_inc(s_qx, 1)

                        def v_stats(j):
                            s2 = 6 * (j % 4)
                            v.wait_ge(s_sq, 2 * j + 2)
                            v.tensor_scalar(st2[:, s2 + 2:s2 + 3],
                                            st2[:, s2 + 0:s2 + 1],
                                            -INV1024, None, OP.mult, OP.bypass)
                            vss.sync(v.tensor_tensor(st2[:, s2 + 3:s2 + 4],
                                                     st2[:, s2 + 0:s2 + 1],
                                                     st2[:, s2 + 0:s2 + 1],
                                                     OP.mult))
                            v.scalar_tensor_tensor(
                                st2[:, s2 + 4:s2 + 5],
                                st2[:, s2 + 3:s2 + 4], -INV1024,
                                st2[:, s2 + 1:s2 + 2], OP.mult,
                                OP.add).then_inc(s_var, 1)

                        def v_evac2(jq):
                            pq = jq % 2
                            v.wait_ge(s_dqt, jq + 1)
                            v.tensor_copy(dq_row[0:1, 1 + jq * 128:
                                                  1 + (jq + 1) * 128],
                                          ps_dq[0:1, :]).then_inc(s_dqe, 1)
                            v.wait_ge(s_trx, jq + 1)
                            if jq >= 2:
                                v.wait_ge(s_spxq[pq], 16 * (jq // 2))
                            v.tensor_copy(xqT_ev[:, pq * C:(pq + 1) * C],
                                          ps_tr2[:, 2 * C:3 * C]
                                          ).then_inc(s_tex, 1)

                        for i in range(NS4):
                            par = i % 2
                            p3 = i % 3
                            if i < NT:
                                v.wait_ge(s_trq, i + 1)
                                if i >= 2:
                                    v.wait_ge(s_mt2, 2 * (i - 2) + 2)
                                v.tensor_copy(q2fT[:, par * C:(par + 1) * C],
                                              ps_tr2[:, 0:C]).then_inc(s_teq, 1)
                            if 0 <= i - 3 < NT:
                                v_quant_a(i - 3)
                            if i < NT:
                                for dh in range(2):
                                    v.wait_ge(s_mt2, 2 * i + dh + 1)
                                    if i >= 2 and dh == 0:
                                        v.wait_ge(s_trv, i - 1)
                                    v.tensor_tensor(
                                        vt2[:, par * C + dh * 512:
                                            par * C + (dh + 1) * 512],
                                        ps_t2[dh][:, 0:512],
                                        v_t2[:, par * C + dh * 512:
                                             par * C + (dh + 1) * 512],
                                        OP.add).then_inc(s_ev2, 1)
                                v.wait_ge(s_trv, i + 1)
                                if i >= 2:
                                    v.wait_ge(s_ma, 2 * (i - 2) + 2)
                                v.tensor_copy(vt2T[:, par * C:(par + 1) * C],
                                              ps_tr2[:, C:2 * C]
                                              ).then_inc(s_tev, 1)
                            if 0 <= i - 3 < NT:
                                v_quant_b(i - 3)
                            if 0 <= i - 1 < NT:
                                v_stats(i - 1)
                            if i < NT:
                                for eh in range(2):
                                    v.wait_ge(s_ma, 2 * i + eh + 1)
                                    if i >= 3 and eh == 0:
                                        v.wait_ge(s_sq, 2 * i - 4)
                                    v.tensor_tensor(
                                        r1_f[:, p3 * C + eh * 512:
                                             p3 * C + (eh + 1) * 512],
                                        ps_at[eh][:, 0:512],
                                        x_f2[:, par * C + eh * 512:
                                             par * C + (eh + 1) * 512],
                                        OP.add).then_inc(s_ev2, 1)
                            if 0 <= i - 4 < NT:
                                v_evac2(i - 4)

                    @blk.scalar
                    def _(sc):
                        def s_rstd(j):
                            s2 = 6 * (j % 4)
                            sc.wait_ge(s_var, j + 1)
                            sc.activation(st2[:, s2 + 5:s2 + 6],
                                          st2[:, s2 + 4:s2 + 5], AF.Sqrt,
                                          bias=eps_col[:, :], scale=INV1024
                                          ).then_inc(s_rs, 1)

                        for i in range(NT):
                            p3 = i % 3
                            s2 = 6 * (i % 4)
                            if i >= 2:
                                s_rstd(i - 2)
                            sc.wait_ge(s_ev2, 4 * i + 4)
                            sc.activation(sqscr[:, :],
                                          r1_f[:, p3 * C:(p3 + 1) * C],
                                          AF.Identity,
                                          accum_out=st2[:, s2 + 0:s2 + 1]
                                          ).then_inc(s_sq, 1)
                            sc.activation(sqscr[:, :],
                                          r1_f[:, p3 * C:(p3 + 1) * C],
                                          AF.Square,
                                          accum_out=st2[:, s2 + 1:s2 + 2]
                                          ).then_inc(s_sq, 1)
                        s_rstd(NT - 2)
                        s_rstd(NT - 1)

                    @blk.gpsimd
                    def _(g):
                        def x2_spill(k):
                            g.wait_ge(s_x2v, k + 1)
                            g.dma_start(x2_s[k * 128:(k + 1) * 128, :],
                                        x2_f[:, (k % 2) * C:(k % 2 + 1) * C]
                                        ).then_inc(s_spx2[k % 2], 16)

                        def xq_spill(k):
                            pk = k % 2
                            g.wait_ge(s_tex, k + 1)
                            g.dma_start(
                                bass.AP(xqT_s, 1 + k * 128,
                                        [[N + 2, 128], [128 * (N + 2), 8],
                                         [1, 128]]),
                                bass.AP(xqT_ev, pk * C,
                                        [[2 * C, 128], [128, 8], [1, 128]]),
                            ).then_inc(s_spxq[pk], 16)

                        for k in range(NT):
                            x2_spill(k)
                            xq_spill(k)
                        for sm in (s_spx2, s_spxq):
                            g.wait_ge(sm[0], 16 * (NT // 2))
                            g.wait_ge(sm[1], 16 * (NT // 2))


        with contextlib.ExitStack() as es5:
            sem5 = sem_scope(nc, es5)
            slab = es5.enter_context(nc.sbuf_tensor("slab", [128, 2 * 4096], BF16))
            xqT_t = es5.enter_context(nc.sbuf_tensor("xqT_t", [128, 2 * 8 * 258], BF16))
            x2_t = es5.enter_context(nc.sbuf_tensor("x2_t", [128, 2 * 2048], F32))
            dqB = es5.enter_context(nc.sbuf_tensor("dqB", [128, 258], F32))
            hpre = es5.enter_context(nc.sbuf_tensor("hpre", [128, 3 * 258], BF16))
            acc3 = es5.enter_context(nc.sbuf_tensor("acc3", [128, 4 * 256], BF16))
            hs = es5.enter_context(nc.sbuf_tensor("hs", [128, 2 * 32 * 256], BF16))
            runmax = es5.enter_context(nc.sbuf_tensor("runmax", [128, 2 * 1024], BF16))
            runminB = es5.enter_context(nc.sbuf_tensor("runminB", [128, 2 * 1024], BF16))
            absB = es5.enter_context(nc.sbuf_tensor("absB", [128, 2 * 256], F32))
            u2b = es5.enter_context(nc.sbuf_tensor("u2b", [128, 1024], F32))
            xib = es5.enter_context(nc.sbuf_tensor("xib", [128, 1024], I32))
            corrb = es5.enter_context(nc.sbuf_tensor("corrb", [128, 512], BF16))
            hq = es5.enter_context(nc.sbuf_tensor("hq", [128, 32 * 256], BF16))
            M2c = es5.enter_context(nc.sbuf_tensor("M2c", [128, 4], F32))
            Mc2 = es5.enter_context(nc.sbuf_tensor("Mc2", [128, 4], F32))
            dqoC = es5.enter_context(nc.sbuf_tensor("dqoC", [128, 4], F32))
            mrow = es5.enter_context(nc.sbuf_tensor("mrow", [1, 2 * 256], F32))
            s2row = es5.enter_context(nc.sbuf_tensor("s2row", [1, 2 * 256], F32))
            out_t = es5.enter_context(nc.sbuf_tensor("out_t", [128, 2048], F32))
            st4 = es5.enter_context(nc.sbuf_tensor("st4", [128, 32], F32))
            ps_h0 = es5.enter_context(nc.psum_tensor("ps_h0", [128, 512], F32))
            ps_h1 = es5.enter_context(nc.psum_tensor("ps_h1", [128, 512], F32))
            ps_o0 = es5.enter_context(nc.psum_tensor("ps_o0", [128, 512], F32))
            ps_o1 = es5.enter_context(nc.psum_tensor("ps_o1", [128, 512], F32))
            ps_o2 = es5.enter_context(nc.psum_tensor("ps_o2", [128, 512], F32))
            ps_o3 = es5.enter_context(nc.psum_tensor("ps_o3", [128, 512], F32))
            ps_mi = es5.enter_context(nc.psum_tensor("ps_mi", [128, 512], F32))
            ps_s2 = es5.enter_context(nc.psum_tensor("ps_s2", [128, 256], F32))
            s_lxq = sem5("d_lxq")
            s_lx2 = sem5("d_lx2")
            s_lsl = [sem5("d_lsl0"),
                     sem5("d_lsl1")]
            s_dqb = sem5("d_dqb")
            s_dqbe = sem5("d_dqbe")
            s_m1 = sem5("d_m1")
            s_hev = sem5("d_hev")
            s_c1 = sem5("d_c1")
            s_c3 = sem5("d_c3")
            s_sil = sem5("d_sil")
            s_rmx = sem5("d_rmx")
            s_rmn = sem5("d_rmn")
            s_s2a = sem5("d_s2a")
            s_str = sem5("d_str")
            s_mc = sem5("d_mc")
            s_mtr = sem5("d_mtr")
            s_s2r = sem5("d_s2r")
            s_msf = sem5("d_msf")
            s_s2b = sem5("d_s2b")
            s_uxi = sem5("d_uxi")
            s_xi = sem5("d_xi")
            s_hq = sem5("d_hq")
            s_m2 = sem5("d_m2")
            s_oev = sem5("d_oev")
            s_lnv = sem5("d_lnv")
            s_ln3 = sem5("d_ln3")
            s_sq3 = sem5("d_sq3")
            s_sqr3 = sem5("d_sqr3")
            s_nrm = sem5("d_nrm")
            s_va = sem5("d_va")
            s_vb = sem5("d_vb")
            s_ost = [sem5("d_ost0"),
                     sem5("d_ost1")]
            s_dvs = sem5("d_dvs")
            ps_mm1 = [ps_h0, ps_h1]
            ps_mm2 = [ps_o0, ps_o1, ps_o2, ps_o3]
            NB = 16
            with nc.Block() as blk:
                @blk.sync
                def _(sy):
                    for i in range(NT3):
                        par = i % 2
                        if i >= 2:
                            sy.wait_ge(s_m1, 32 * (i - 1))
                        sy.dma_start(
                            bass.AP(xqT_t, par * 8 * 258,
                                    [[2 * 8 * 258, 128], [258, 8], [1, 258]]),
                            bass.AP(xqT_s, i * 256,
                                    [[N + 2, 128], [128 * (N + 2), 8], [1, 258]]),
                        ).then_inc(s_lxq, 16)
                        for s8 in range(8):
                            gsl = 8 * i + s8
                            if gsl >= 2:
                                sy.wait_ge(s_m1, 4 * (gsl - 1))
                            sy.dma_start(
                                bass.AP(slab, (gsl % 2) * 4096,
                                        [[2 * 4096, 128], [512, 8], [1, 512]]),
                                bass.AP(t1T_s, s8 * 512,
                                        [[H, 128], [128 * H, 8], [1, 512]]),
                            ).then_inc(s_lsl[gsl % 2], 16)
                        if i >= 2:
                            sy.wait_ge(s_nrm, 2 * i - 2)
                        sy.dma_start(
                            bass.AP(x2_t, (i % 2) * 2048,
                                    [[2 * 2048, 128], [1024, 2], [1, 1024]]),
                            bass.AP(x2_s, i * 256 * C,
                                    [[C, 128], [128 * C, 2], [1, 1024]]),
                        ).then_inc(s_lx2, 16)

                @blk.tensor
                def _(t):
                    def mm1_slabs(i, s8lo, s8hi):
                        for s8 in range(s8lo, s8hi):
                            gsl = 8 * i + s8
                            t.wait_ge(s_lsl[gsl % 2], 16 * (gsl // 2 + 1))
                            for cc in range(4):
                                hc = 4 * s8 + cc
                                if 32 * i + hc >= 2:
                                    t.wait_ge(s_hev, 32 * i + hc - 1)
                                ins = None
                                for j in range(8):
                                    ins = t.matmul(
                                        ps_mm1[hc % 2][:, 0:258],
                                        slab[:, (gsl % 2) * 4096 +
                                             j * 512 + cc * 128:
                                             (gsl % 2) * 4096 +
                                             j * 512 + (cc + 1) * 128],
                                        xqT_t[:, (i % 2) * 8 * 258 + j * 258:
                                              (i % 2) * 8 * 258 +
                                              (j + 1) * 258],
                                        start=(j == 0), stop=(j == 7))
                                ins.then_inc(s_m1, 1)

                    for i in range(NT3 + 1):
                        if i < NT3:
                            if i >= 1:
                                t.wait_ge(s_dqbe, i)
                            if i >= 2:
                                t.wait_ge(s_mc, i - 1)
                                t.wait_ge(s_msf, i - 1)
                            t.matmul(ps_mi[:, 0:258], ones_row_f[0:1, :],
                                     dq_row[0:1, i * 256:i * 256 + 258],
                                     start=True, stop=True).then_inc(s_dqb, 1)
                        if i >= 1:
                            jj = i - 1
                            pj = jj % 2
                            t.wait_ge(s_s2a, jj + 1)
                            t.wait_ge(s_dqbe, min(jj + 2, NT3))
                            ins = None
                            for k in range(2):
                                ins = t.transpose(
                                    ps_mi[:, k * 128:(k + 1) * 128],
                                    absB[:, pj * 256 + k * 128:
                                         pj * 256 + (k + 1) * 128],
                                    ident_f[:, :])
                            ins.then_inc(s_str, 1)
                            t.wait_ge(s_mc, jj + 1)
                            ins = None
                            for k in range(2):
                                ins = t.transpose(
                                    ps_mi[0:1, 256 + k * 128:256 + (k + 1) * 128],
                                    Mc2[:, 2 * pj + k:2 * pj + k + 1],
                                    ident_f[:, :])
                            ins.then_inc(s_mtr, 1)
                            t.wait_ge(s_s2r, jj + 1)
                            if jj >= 1:
                                t.wait_ge(s_uxi, NB * jj)
                            t.matmul(ps_s2[:, 0:256], ones_row_f[0:1, :],
                                     s2row[0:1, pj * 256:(pj + 1) * 256],
                                     start=True, stop=True).then_inc(s_s2b, 1)
                            if jj >= 1:
                                t.wait_ge(s_oev, 2 * jj)
                            for hc in range(32):
                                t.wait_ge(s_hq, NB * jj + hc // 2 + 1)
                                ins = None
                                for sub in range(2):
                                    for eh in range(2):
                                        ins = t.matmul(
                                            ps_mm2[2 * sub + eh][:, 0:512],
                                            hq[:, hc * 256 + sub * 128:
                                               hc * 256 + (sub + 1) * 128],
                                            t2w_sb[:, hc * 1024 + eh * 512:
                                                   hc * 1024 + (eh + 1) * 512],
                                            start=(hc == 0), stop=(hc == 31))
                                if hc == 31:
                                    ins.then_inc(s_m2, 1)
                        if i < NT3:
                            t.wait_ge(s_lxq, 16 * (i + 1))
                            mm1_slabs(i, 0, 8)

                @blk.vector
                def _(v):
                    vss = SelfSync(v, s_dvs)
                    for i in range(NT3 + 1):
                        if i < NT3:
                            par = i % 2
                            v.wait_ge(s_dqb, i + 1)
                            v.tensor_copy(dqB[:, :], ps_mi[:, 0:258]
                                          ).then_inc(s_dqbe, 1)
                        if i >= 2:
                            jb = i - 2
                            s4b = 16 * (jb % 2)
                            for sub in range(2):
                                s4 = s4b + 8 * sub
                                v.wait_ge(s_sqr3, 2 * jb + sub + 1)
                                vss.sync(v.reciprocal(st4[:, s4 + 6:s4 + 7],
                                                      st4[:, s4 + 5:s4 + 6]))
                                if jb >= 2:
                                    v.wait_ge(s_nrm, 2 * jb - 2)
                                vss.sync(v.scalar_tensor_tensor(
                                    st4[:, s4 + 7:s4 + 8],
                                    st4[:, s4 + 1:s4 + 2], -1.0,
                                    st4[:, s4 + 6:s4 + 7], OP.mult, OP.mult))
                                v.nop().then_inc(s_vb, 1)
                        if i >= 1:
                            jj = i - 1
                            pj = jj % 2
                            v.wait_ge(s_str, jj + 1)
                            for k in range(2):
                                vss.sync(v.reduce_max(
                                    M2c[:, 2 * pj + k:2 * pj + k + 1],
                                    ps_mi[:, k * 128:(k + 1) * 128], AX.X))
                            vss.sync(v.tensor_scalar(
                                Mc2[:, 2 * pj:2 * pj + 2],
                                M2c[:, 2 * pj:2 * pj + 2], QCLIP, None,
                                OP.max, OP.bypass))
                            vss.sync(v.tensor_scalar(
                                dqoC[:, 2 * pj:2 * pj + 2],
                                Mc2[:, 2 * pj:2 * pj + 2], m2_col[:, :],
                                INV127, OP.mult, OP.mult))
                            v.nop().then_inc(s_mc, 1)
                            v.wait_ge(s_mtr, jj + 1)
                            v.tensor_copy(mrow[0:1, pj * 256:(pj + 1) * 256],
                                          ps_mi[0:1, 256:512]
                                          ).then_inc(s_msf, 1)
                            vss.sync(v.reciprocal(
                                s2row[0:1, pj * 256:(pj + 1) * 256],
                                mrow[0:1, pj * 256:(pj + 1) * 256]))
                            v.tensor_scalar(s2row[0:1, pj * 256:(pj + 1) * 256],
                                            s2row[0:1, pj * 256:(pj + 1) * 256],
                                            254.0, None, OP.mult, OP.bypass
                                            ).then_inc(s_s2r, 1)
                            v.wait_ge(s_s2b, jj + 1)
                            if jj >= 1:
                                v.wait_ge(s_m2, jj)
                            for bq in range(NB + 1):
                                if bq < NB:
                                    bs = (bq % 2) * 512
                                    if jj >= 1 or bq >= 2:
                                        v.wait_ge(s_xi, NB * jj + bq - 1)
                                    ins = None
                                    for cc in range(2):
                                        hc = 2 * bq + cc
                                        hsl = hs[:, pj * 8192 + hc * 256:
                                                 pj * 8192 + (hc + 1) * 256]
                                        ins = v.tensor_tensor(
                                            u2b[:, bs + cc * 256:
                                                bs + (cc + 1) * 256],
                                            hsl, ps_s2[:, 0:256], OP.mult)
                                    ins.then_inc(s_uxi, 1)
                                if bq >= 1:
                                    bp = bq - 1
                                    bs = (bp % 2) * 512
                                    u2s = u2b[:, bs:bs + 512]
                                    xis = xib[:, bs:bs + 512]
                                    v.wait_ge(s_xi, NB * jj + bp + 1)
                                    v.scalar_tensor_tensor(u2s, xis,
                                                           -2.0, u2s,
                                                           OP.mult, OP.add)
                                    v.scalar_tensor_tensor(corrb[:, :], u2s,
                                                           -1.0, xis,
                                                           OP.is_lt, OP.subtract)
                                    v.scalar_tensor_tensor(
                                        hq[:, bp * 512:(bp + 1) * 512],
                                        u2s, 1.0, corrb[:, :],
                                        OP.is_gt, OP.subtract).then_inc(s_hq, 1)
                            v.wait_ge(s_m2, jj + 1)
                            v.wait_ge(s_lx2, 16 * (jj + 1))
                            xb = (jj % 2) * 2048
                            for sub in range(2):
                                if jj >= 2:
                                    v.wait_ge(s_nrm, 2 * (jj - 2) + sub + 1)
                                ins = None
                                for eh in range(2):
                                    ins = v.scalar_tensor_tensor(
                                        x2_t[:, xb + sub * 1024 + eh * 512:
                                             xb + sub * 1024 + (eh + 1) * 512],
                                        ps_mm2[2 * sub + eh][:, 0:512],
                                        dqoC[:, 2 * pj + sub:2 * pj + sub + 1],
                                        x2_t[:, xb + sub * 1024 + eh * 512:
                                             xb + sub * 1024 + (eh + 1) * 512],
                                        OP.mult, OP.add)
                                ins.then_inc(s_oev, 1)

                        if i < NT3:
                            par = i % 2
                            for hcs in range(33):
                                if hcs < 32:
                                    hc = hcs
                                    v.wait_ge(s_m1, 32 * i + hc + 1)
                                    v.tensor_tensor(hpre[:, (hc % 3) * 258:
                                                         (hc % 3 + 1) * 258],
                                                    ps_mm1[hc % 2][:, 0:258],
                                                    dqB[:, :], OP.mult
                                                    ).then_inc(s_hev, 1)
                                if hcs >= 1:
                                    hc = hcs - 1
                                    g = 32 * i + hc
                                    v.wait_ge(s_c1, g + 1)
                                    a3 = acc3[:, (hc % 4) * 256:
                                              (hc % 4 + 1) * 256]
                                    v.scalar_tensor_tensor(
                                        a3,
                                        hpre[:, (hc % 3) * 258:
                                             (hc % 3) * 258 + 256],
                                        cw_sb[:, 3 * hc:3 * hc + 1],
                                        a3, OP.mult, OP.add)
                                    v.scalar_tensor_tensor(
                                        a3,
                                        hpre[:, (hc % 3) * 258 + 2:
                                             (hc % 3) * 258 + 258],
                                        cw_sb[:, 3 * hc + 2:3 * hc + 3],
                                        a3, OP.mult, OP.add).then_inc(s_c3, 1)
                            for bq4 in range(8):
                                v.wait_ge(s_sil, 32 * i + 4 * bq4 + 4)
                                hb = hs[:, par * 8192 + bq4 * 1024:
                                        par * 8192 + (bq4 + 1) * 1024]
                                rx = runmax[:, par * 1024:(par + 1) * 1024]
                                rn = runminB[:, par * 1024:(par + 1) * 1024]
                                if bq4 == 0:
                                    v.tensor_copy(rx, hb).then_inc(s_rmx, 1)
                                    v.tensor_copy(rn, hb).then_inc(s_rmn, 1)
                                else:
                                    v.tensor_tensor(rx, rx, hb, OP.max
                                                    ).then_inc(s_rmx, 1)
                                    v.tensor_tensor(rn, rn, hb, OP.min
                                                    ).then_inc(s_rmn, 1)
                            if i >= 2:
                                v.wait_ge(s_str, i - 1)
                            rb = runminB[:, par * 1024:(par + 1) * 1024]
                            rx = runmax[:, par * 1024:(par + 1) * 1024]
                            ab = absB[:, par * 256:(par + 1) * 256]
                            v.tensor_tensor(ab, rb[:, 0:256], rb[:, 256:512],
                                            OP.min)
                            v.tensor_tensor(ab, ab, rb[:, 512:768], OP.min)
                            v.tensor_tensor(ab, ab, rb[:, 768:1024], OP.min)
                            v.tensor_scalar(ab, ab, -1.0, None,
                                            OP.mult, OP.bypass)
                            v.tensor_tensor(ab, ab, rx[:, 0:256], OP.max)
                            v.tensor_tensor(ab, ab, rx[:, 256:512], OP.max)
                            v.tensor_tensor(ab, ab, rx[:, 512:768], OP.max)
                            v.tensor_tensor(ab, ab, rx[:, 768:1024],
                                            OP.max).then_inc(s_s2a, 1)
                        if i >= 1:
                            ja = i - 1
                            xb2 = (ja % 2) * 2048
                            s4b = 16 * (ja % 2)
                            for sub in range(2):
                                s4 = s4b + 8 * sub
                                v.wait_ge(s_sq3, 2 * ja + sub + 1)
                                vss.sync(v.tensor_scalar(
                                    st4[:, s4 + 1:s4 + 2],
                                    st4[:, s4 + 0:s4 + 1], INV1024, None,
                                    OP.mult, OP.bypass))
                                vss.sync(v.tensor_tensor(
                                    st4[:, s4 + 3:s4 + 4],
                                    st4[:, s4 + 1:s4 + 2],
                                    st4[:, s4 + 1:s4 + 2], OP.mult))
                                vss.sync(v.scalar_tensor_tensor(
                                    st4[:, s4 + 4:s4 + 5],
                                    st4[:, s4 + 2:s4 + 3], INV1024,
                                    st4[:, s4 + 3:s4 + 4],
                                    OP.mult, OP.subtract))
                                vss.sync(v.tensor_scalar(
                                    st4[:, s4 + 4:s4 + 5],
                                    st4[:, s4 + 4:s4 + 5], LN_EPS, None,
                                    OP.add, OP.bypass))
                                v.nop().then_inc(s_va, 1)
                    jb = NT3 - 1
                    s4b = 16 * (jb % 2)
                    for sub in range(2):
                        s4 = s4b + 8 * sub
                        v.wait_ge(s_sqr3, 2 * jb + sub + 1)
                        vss.sync(v.reciprocal(st4[:, s4 + 6:s4 + 7],
                                              st4[:, s4 + 5:s4 + 6]))
                        vss.sync(v.scalar_tensor_tensor(
                            st4[:, s4 + 7:s4 + 8],
                            st4[:, s4 + 1:s4 + 2], -1.0,
                            st4[:, s4 + 6:s4 + 7], OP.mult, OP.mult))
                        v.nop().then_inc(s_vb, 1)

                @blk.scalar
                def _(sc):
                    def s_sqrt3(js):
                        s4b = 16 * (js % 2)
                        for sub in range(2):
                            s4 = s4b + 8 * sub
                            sc.wait_ge(s_va, 2 * js + sub + 1)
                            sc.activation(st4[:, s4 + 5:s4 + 6],
                                          st4[:, s4 + 4:s4 + 5], AF.Sqrt
                                          ).then_inc(s_sqr3, 1)

                    def s_norm3(jn):
                        xb = (jn % 2) * 2048
                        s4b = 16 * (jn % 2)
                        for sub in range(2):
                            s4 = s4b + 8 * sub
                            sc.wait_ge(s_vb, 2 * jn + sub + 1)
                            if jn >= 1:
                                sc.wait_ge(s_ost[sub], 16 * jn)
                            sc.activation(
                                out_t[:, sub * 1024:(sub + 1) * 1024],
                                x2_t[:, xb + sub * 1024:
                                     xb + (sub + 1) * 1024],
                                AF.Identity,
                                bias=st4[:, s4 + 7:s4 + 8],
                                scale=st4[:, s4 + 6:s4 + 7]
                                ).then_inc(s_nrm, 1)
                            sc.wait_ge(s_nrm, 2 * jn + sub + 1)
                            sc.dma_start(
                                out_h[jn * 256 + sub * 128:
                                      jn * 256 + (sub + 1) * 128, :],
                                out_t[:, sub * 1024:(sub + 1) * 1024]
                            ).then_inc(s_ost[sub], 16)

                    for i in range(NT3 + 1):
                        if i >= 2:
                            s_sqrt3(i - 2)
                        if i >= 1:
                            jj = i - 1
                            for bq in range(NB):
                                bs = (bq % 2) * 512
                                sc.wait_ge(s_uxi, NB * jj + bq + 1)
                                sc.activation(xib[:, bs:bs + 512],
                                              u2b[:, bs:bs + 512],
                                              AF.Identity, scale=0.5
                                              ).then_inc(s_xi, 1)
                            xb = (jj % 2) * 2048
                            s4b = 16 * (jj % 2)
                            for sub in range(2):
                                s4 = s4b + 8 * sub
                                sc.wait_ge(s_oev, 2 * jj + sub + 1)
                                if jj >= 2:
                                    sc.wait_ge(s_ost[sub], 16 * (jj - 1))
                                sc.activation(
                                    out_t[:, sub * 1024:(sub + 1) * 1024],
                                    x2_t[:, xb + sub * 1024:
                                         xb + (sub + 1) * 1024],
                                    AF.Identity,
                                    accum_out=st4[:, s4 + 0:s4 + 1])
                                sc.activation(
                                    out_t[:, sub * 1024:(sub + 1) * 1024],
                                    x2_t[:, xb + sub * 1024:
                                         xb + (sub + 1) * 1024],
                                    AF.Square,
                                    accum_out=st4[:, s4 + 2:s4 + 3]
                                    ).then_inc(s_sq3, 1)
                        if i < NT3:
                            par = i % 2
                            for hcs in range(33):
                                if hcs < 32:
                                    hc = hcs
                                    g = 32 * i + hc
                                    sc.wait_ge(s_hev, g + 1)
                                    if g >= 4:
                                        sc.wait_ge(s_sil, g - 3)
                                    sc.activation(
                                        acc3[:, (hc % 4) * 256:
                                             (hc % 4 + 1) * 256],
                                        hpre[:, (hc % 3) * 258 + 1:
                                             (hc % 3) * 258 + 257],
                                        AF.Identity,
                                        scale=cw_sb[:, 3 * hc + 1:3 * hc + 2],
                                        bias=cb_sb[:, hc:hc + 1]
                                        ).then_inc(s_c1, 1)
                                if hcs >= 1:
                                    hc = hcs - 1
                                    sc.wait_ge(s_c3, 32 * i + hc + 1)
                                    if hc == 0 and i >= 2:
                                        sc.wait_ge(s_hq, NB * (i - 1))
                                        sc.wait_ge(s_rmx, 8 * (i - 1))
                                        sc.wait_ge(s_rmn, 8 * (i - 1))
                                    sc.activation(
                                        hs[:, par * 8192 + hc * 256:
                                           par * 8192 + (hc + 1) * 256],
                                        acc3[:, (hc % 4) * 256:
                                             (hc % 4 + 1) * 256],
                                        AF.Silu).then_inc(s_sil, 1)
                        if i >= 2:
                            s_norm3(i - 2)
                    s_sqrt3(NT3 - 1)
                    s_norm3(NT3 - 1)
                    sc.wait_ge(s_ost[0], 16 * NT3)
                    sc.wait_ge(s_ost[1], 16 * NT3)

    return nc


_NC_CACHE = {}


def kernel(L0_attn_x, L0_attn_Wqkv, L0_attn_bqkv, L0_attn_Wln, L0_attn_bln,
           L0_ffn_w1, L0_ffn_conv_w, L0_ffn_conv_b, L0_ffn_w2):
    B = L0_attn_x.shape[0]
    x = np.ascontiguousarray(np.asarray(L0_attn_x, dtype=np.float32))
    wqkv = np.ascontiguousarray(np.asarray(L0_attn_Wqkv, dtype=np.float32))
    bqkv = np.ascontiguousarray(np.asarray(L0_attn_bqkv,
                                           dtype=np.float32).reshape(1, M3))
    wln = np.ascontiguousarray(np.asarray(L0_attn_Wln, dtype=np.float32))
    bln = np.ascontiguousarray(np.asarray(L0_attn_bln,
                                          dtype=np.float32).reshape(1, C))
    w1 = np.ascontiguousarray(np.asarray(L0_ffn_w1, dtype=np.float32))
    cw = np.ascontiguousarray(
        np.asarray(L0_ffn_conv_w, dtype=np.float32).reshape(32, 128, 3)
        .transpose(1, 0, 2).reshape(128, 96))
    cb = np.ascontiguousarray(
        np.asarray(L0_ffn_conv_b, dtype=np.float32).reshape(32, 128).T)
    w2 = np.ascontiguousarray(np.asarray(L0_ffn_w2, dtype=np.float32))

    if "nc" not in _NC_CACHE:
        _NC_CACHE["nc"] = build_nc()
    nc = _NC_CACHE["nc"]

    in_maps = []
    for b in range(B):
        in_maps.append({
            "x": np.ascontiguousarray(x[b]),
            "wqkv": wqkv, "bqkv": bqkv, "wln": wln, "bln": bln,
            "w1": w1, "cw": cw, "cb": cb, "w2": w2,
        })
    res = run_bass_kernel_spmd(nc, in_maps, core_ids=list(range(B)))
    out = np.stack([np.asarray(r["out"]) for r in res.results], axis=0)
    return out
